# version 28
# baseline (speedup 1.0000x reference)
import numpy as np
import concourse.bass as bass
import concourse.mybir as mybir

F32 = mybir.dt.float32
BF16 = mybir.dt.bfloat16
AF = mybir.ActivationFunctionType
ALU = mybir.AluOpType

D = 1024
DI = 2048
NH = 32
DPROJ = 4384
L = 128
EPS = 1e-5
NKD = 8
NMP = 35
Z_OFF = 16
B_TILE = 32
C_TILE = 33
DT_TILE = 34
PROJ_ORDER = list(range(16)) + [32, 33, 34] + list(range(16, 32))
PSA, PCA, PGA, PSB, PCB, PGB = 0, 8, 16, 24, 32, 40



_OPS = ("tensor_tensor", "tensor_scalar", "tensor_copy", "scalar_tensor_tensor",
        "tensor_tensor_scan", "memset", "activation", "reciprocal", "iota",
        "affine_select", "tensor_reduce", "reduce_sum", "reduce_max", "select",
        "tensor_single_scalar", "copy", "add", "mul", "square", "sign",
        "matmul", "transpose", "nop")


import os
_SKIP = set(filter(None, os.environ.get("MK_SKIP", "").split(",")))


class SkipCompute:
    def __init__(self, eng):
        self._e = eng

    def __getattr__(self, n):
        a = getattr(self._e, n)
        if n in _OPS + ("ldweights",):
            eng = self._e

            def w(*args, **kw):
                return eng.nop()
            return w
        return a


class SkipDma:
    def __init__(self, eng):
        self._e = eng

    def __getattr__(self, n):
        a = getattr(self._e, n)
        if n == "dma_start":
            eng = self._e

            def w(*args, **kw):
                class FakeInst:
                    def then_inc(self, sem, v):
                        return eng.sem_inc(sem, v)
                return FakeInst()
            return w
        return a


class Rec:

    def __init__(self, eng):
        self._e = eng
        self.last = None

    def __getattr__(self, n):
        a = getattr(self._e, n)
        if n in _OPS:
            rec = self

            def w(*args, **kw):
                r = a(*args, **kw)
                rec.last = r
                return r
            return w
        return a

    def inc(self, sem, v=1):
        return self.last.then_inc(sem, v)


class Sched:
    def __init__(self, prologue_tags, chunk_tags):
        self.pt = {t: i + 1 for i, t in enumerate(prologue_tags)}
        self.P = len(prologue_tags)
        self.ct = {t: i + 1 for i, t in enumerate(chunk_tags)}
        self.N = len(chunk_tags)

    def pv(self, tag):
        return self.pt[tag]

    def v(self, c, tag):
        return self.P + c * self.N + self.ct[tag]


def build_nc(NCH):
    nc = bass.Bass("TRN2", target_bir_lowering=False)

    xin = nc.dram_tensor("x", [NCH, D, L], F32, kind="ExternalInput").ap()
    xout = nc.dram_tensor("out", [NCH, D, L], F32, kind="ExternalOutput").ap()
    wipt = nc.dram_tensor("wipt", [NKD, 128, DPROJ], F32, kind="ExternalInput").ap()
    wopt = nc.dram_tensor("wopt", [16, 128, D], F32, kind="ExternalInput").ap()
    wrt = nc.dram_tensor("wrt", [NKD, 128, D], F32, kind="ExternalInput").ap()
    wkt = nc.dram_tensor("wkt", [NKD, 128, D], F32, kind="ExternalInput").ap()
    wvt = nc.dram_tensor("wvt", [NKD, 128, D], F32, kind="ExternalInput").ap()
    wmodb = nc.dram_tensor("wmodb", [NKD, 3, 128, 2048], F32, kind="ExternalInput").ap()
    crow_d = nc.dram_tensor("crow", [128, 8], F32, kind="ExternalInput").ap()
    alog_d = nc.dram_tensor("alog", [32, 1], F32, kind="ExternalInput").ap()
    dp_d = nc.dram_tensor("dpv", [32, 1], F32, kind="ExternalInput").ap()
    dtb_d = nc.dram_tensor("dtb", [32, 1], F32, kind="ExternalInput").ap()
    convw_d = nc.dram_tensor("convw", [128, 64], F32, kind="ExternalInput").ap()
    convb_d = nc.dram_tensor("convb", [128, 16], F32, kind="ExternalInput").ap()
    normw_d = nc.dram_tensor("normw", [128, 16], F32, kind="ExternalInput").ap()
    mur_d = nc.dram_tensor("mur", [128, 8], F32, kind="ExternalInput").ap()
    muk_d = nc.dram_tensor("muk", [128, 8], F32, kind="ExternalInput").ap()
    modscr = nc.dram_tensor("modscr", [1, 6144], F32, kind="Internal").ap()

    pe_p = [f"mod{nb}_{k}" for nb in range(3) for k in range(NKD)] + ["dpbc", "modt"]
    pe_c = (["stats", "bcast"] + [f"proj{m}" for m in PROJ_ORDER[:19]] + ["g"]
            + [f"proj{m}" for m in PROJ_ORDER[19:]] + ["dtt", "xt1", "xt2"]
            + ["bc0", "w3b0", "bc1"]
            + sum([[f"y{o}", f"st{o}"]
                   + ([f"w3b{o + 1}"] if o <= 6 else [])
                   + ([f"bc{o + 2}"] if o <= 5 else [])
                   + ([f"rsred{o // 2}"] if o % 2 == 1 else [])
                   for o in range(8)], [])
            + ["rsbc"] + [f"op{m}" for m in range(8)] + ["stats2", "bcast2"]
            + [f"r{m}" for m in range(8)] + [f"k{m}" for m in range(8)]
            + [f"v{m}" for m in range(8)])
    PE = Sched(pe_p, pe_c)

    ve_p = ["dpb", "dpc", "params", "wopfold", "mu1"]
    ve_c = (["tailcp", "lnrows", "lnapply"]
            + ["tapp0", "tapp1", "cum", "hilo", "gmbsn", "xdt", "xtev1", "xtev2", "dpxv",
               "newtail"]
            + sum([[f"s{o}", f"bsc{o}"]
                   + ([f"stev{o - 1}"] if o >= 1 else [])
                   + [f"ctil{o}"]
                   + ([f"ys{o // 2}"] if o % 2 == 1 else [])
                   for o in range(8)], [])
            + ["stev7", "yn"] + [f"x2_{m}" for m in range(8)]
            + ["ln2rows", "ln2apply", "d1", "xr", "xk", "ka2"]
            + [f"vsig{m}" for m in range(8)] + [f"x3_{m}" for m in range(8)])
    VE = Sched(ve_p, ve_c)

    ac_p = ["silc"] + [f"mr{i}" for i in range(12)] + ["acol", "gclamp", "sq0"]
    ac_c = (["mqrow", "rstd", "xpp0", "t00", "xpp1", "t01", "bcev", "dte", "dt2"]
            + [f"z{j}" for j in range(8)] + [f"xc{j}" for j in range(8)]
            + [f"z{j}" for j in range(8, 16)] + [f"xc{j}" for j in range(8, 16)]
            + sum([[f"e{o}", f"ew3{o}", f"w4a{o}"] + ([f"ysq{o // 2}"] if o % 2 == 1 else [])
                   for o in range(8)], [])
            + ["rs", "rse", "sq2", "cp2", "mqrow2", "rstd2", "sqn"]
            + [f"sig{m}" for m in range(8)] + [f"rel{m}" for m in range(8)])
    AC = Sched(ac_p, ac_c)

    NSMALL = 9

    def dgp_wmod(nb, k):
        return NSMALL + nb * NKD + k + 1

    DGP_WIP0 = NSMALL + 24
    DGP_WOP0 = DGP_WIP0 + NKD
    DGP_FFN0 = DGP_WOP0 + 4
    DGP_DPF = DGP_FFN0 + 6 + 1
    DGP_P = DGP_DPF
    DGP_C = 2

    def dgp_v(i):
        return 16 * i

    def dgp_cv(c, i):
        return 16 * (DGP_P + c * DGP_C + i)

    def proj_evac(m):
        if m < 16:
            return ("ve", f"xp{m}")
        if m < 32:
            return ("ac", f"z{m - 16}")
        if m in (32, 33):
            return ("ac", "bcev")
        return ("ac", "dte")

    def mmslot(m):
        return (m % 2) * 512 + ((m // 2) % 4) * 128

    import contextlib
    _st = contextlib.ExitStack()
    s_pe = _st.enter_context(nc.semaphore("s_pe"))
    s_ve = _st.enter_context(nc.semaphore("s_ve"))
    s_ac = _st.enter_context(nc.semaphore("s_ac"))
    s_gp = _st.enter_context(nc.semaphore("s_gp"))
    s_din = _st.enter_context(nc.semaphore("s_din"))
    s_dout = _st.enter_context(nc.semaphore("s_dout"))
    s_dsc = _st.enter_context(nc.semaphore("s_dsc"))
    s_dgp = _st.enter_context(nc.semaphore("s_dgp"))
    s_dsm = _st.enter_context(nc.semaphore("s_dsm"))
    s_dwm0 = _st.enter_context(nc.semaphore("s_dwm0"))
    s_dwm1 = _st.enter_context(nc.semaphore("s_dwm1"))
    s_dwip = _st.enter_context(nc.semaphore("s_dwip"))
    s_dwop = _st.enter_context(nc.semaphore("s_dwop"))
    s_dffn = _st.enter_context(nc.semaphore("s_dffn"))
    s_dpfl = _st.enter_context(nc.semaphore("s_dpfl"))
    s_dfl = _st.enter_context(nc.semaphore("s_dfl"))
    s_din1 = _st.enter_context(nc.semaphore("s_din1"))
    WIP = _st.enter_context(nc.sbuf_tensor("WIP", [128, NKD * DPROJ], BF16))
    WOP = _st.enter_context(nc.sbuf_tensor("WOP", [128, 16 * D], BF16))
    WR = _st.enter_context(nc.sbuf_tensor("WR", [128, NKD * D], BF16))
    WK = _st.enter_context(nc.sbuf_tensor("WK", [128, NKD * D], BF16))
    WV = _st.enter_context(nc.sbuf_tensor("WV", [128, NKD * D], BF16))
    CROW = _st.enter_context(nc.sbuf_tensor("CROW", [128, 8], F32))
    SILC = _st.enter_context(nc.sbuf_tensor("SILC", [128, 8], BF16))
    MODC = _st.enter_context(nc.sbuf_tensor("MODC", [128, 48], F32))
    MODCT = _st.enter_context(nc.sbuf_tensor("MODCT", [128, 128], F32))
    IDENTF = _st.enter_context(nc.sbuf_tensor("IDENTF", [128, 128], F32))
    PARAMS = _st.enter_context(nc.sbuf_tensor("PARAMS", [128, 48], F32))
    ALOG = _st.enter_context(nc.sbuf_tensor("ALOG", [32, 1], F32))
    ACOL = _st.enter_context(nc.sbuf_tensor("ACOL", [32, 1], F32))
    DTB = _st.enter_context(nc.sbuf_tensor("DTB", [32, 1], F32))
    DPIN = _st.enter_context(nc.sbuf_tensor("DPIN", [32, 1], F32))
    DPFLAT = _st.enter_context(nc.sbuf_tensor("DPFLAT", [1, 32], BF16))
    DPB = _st.enter_context(nc.sbuf_tensor("DPB", [128, 32], F32))
    DPC = _st.enter_context(nc.sbuf_tensor("DPC", [128, 16], F32))
    CW = _st.enter_context(nc.sbuf_tensor("CW", [128, 64], F32))
    CONVB = _st.enter_context(nc.sbuf_tensor("CONVB", [128, 16], F32))
    NORMW = _st.enter_context(nc.sbuf_tensor("NORMW", [128, 16], F32))
    MUR1 = _st.enter_context(nc.sbuf_tensor("MUR1", [128, 8], F32))
    MUK1 = _st.enter_context(nc.sbuf_tensor("MUK1", [128, 8], F32))
    ONESB = _st.enter_context(nc.sbuf_tensor("ONESB", [1, 128], BF16))
    ONES1F = _st.enter_context(nc.sbuf_tensor("ONES1F", [1, 128], F32))
    ONES128 = _st.enter_context(nc.sbuf_tensor("ONES128", [128, 1], BF16))
    IDENTB = _st.enter_context(nc.sbuf_tensor("IDENTB", [128, 128], BF16))
    MASK01 = _st.enter_context(nc.sbuf_tensor("MASK01", [128, 128], BF16))
    MASKNEG = _st.enter_context(nc.sbuf_tensor("MASKNEG", [128, 128], BF16))
    ZERO32 = _st.enter_context(nc.sbuf_tensor("ZERO32", [32, 128], F32))
    W4Q = _st.enter_context(nc.sbuf_tensor("W4Q", [128, 8], F32))
    CONSTS = _st.enter_context(nc.sbuf_tensor("CONSTS", [128, 4], F32))
    PACK = _st.enter_context(nc.sbuf_tensor("PACK", [2, 4096], BF16))
    ONES2X = _st.enter_context(nc.sbuf_tensor("ONES2X", [2, 128], BF16))
    CONVTAIL = _st.enter_context(nc.sbuf_tensor("CONVTAIL", [128, 48], BF16))
    TAILH2 = _st.enter_context(nc.sbuf_tensor("TAILH2", [128, 16], BF16))
    RSF = _st.enter_context(nc.sbuf_tensor("RSF", [128, 128], BF16))
    STATE = _st.enter_context(nc.sbuf_tensor("STATE", [128, 2048], BF16))
    X0 = _st.enter_context(nc.sbuf_tensor("X0", [128, 1024], F32))
    X1 = _st.enter_context(nc.sbuf_tensor("X1", [128, 1024], F32))
    HB = _st.enter_context(nc.sbuf_tensor("HB", [128, 1024], BF16))
    SCR2 = _st.enter_context(nc.sbuf_tensor("SCR2", [128, 2048], BF16))
    XP = _st.enter_context(nc.sbuf_tensor("XP", [128, 2096], BF16))
    XC = _st.enter_context(nc.sbuf_tensor("XC", [128, 2048], BF16))
    XT = _st.enter_context(nc.sbuf_tensor("XT", [128, 2048], BF16))
    SZ = _st.enter_context(nc.sbuf_tensor("SZ", [128, 2048], BF16))
    BNT = _st.enter_context(nc.sbuf_tensor("BNT", [128, 128], BF16))
    CNT = _st.enter_context(nc.sbuf_tensor("CNT", [128, 128], BF16))
    BSN = _st.enter_context(nc.sbuf_tensor("BSN", [128, 128], BF16))
    GMT = _st.enter_context(nc.sbuf_tensor("GMT", [128, 128], BF16))
    EQ = _st.enter_context(nc.sbuf_tensor("EQ", [128, 1024], BF16))
    YS = _st.enter_context(nc.sbuf_tensor("YS", [128, 2048], BF16))
    RSBS = _st.enter_context(nc.sbuf_tensor("RSBS", [128, 128], F32))
    STATS = _st.enter_context(nc.sbuf_tensor("STATS", [1, 512], F32))
    E32 = _st.enter_context(nc.sbuf_tensor("E32", [32, 128], F32))
    DT32 = _st.enter_context(nc.sbuf_tensor("DT32", [32, 128], F32))
    CUM = _st.enter_context(nc.sbuf_tensor("CUM", [32, 128], F32))
    CUMHI = _st.enter_context(nc.sbuf_tensor("CUMHI", [32, 128], BF16))
    CUMLO = _st.enter_context(nc.sbuf_tensor("CUMLO", [32, 128], BF16))
    DTT = _st.enter_context(nc.sbuf_tensor("DTT", [128, 32], BF16))
    CUMT = _st.enter_context(nc.sbuf_tensor("CUMT", [128, 32], F32))
    PS_Y = _st.enter_context(nc.psum_tensor("PS_Y", [128, 512], F32))
    PS_BC = _st.enter_context(nc.psum_tensor("PS_BC", [128, 1024], F32))
    PS_MM = _st.enter_context(nc.psum_tensor("PS_MM", [128, 1024], F32))
    PS_ST = _st.enter_context(nc.psum_tensor("PS_ST", [128, 512], F32))
    PS_T = _st.enter_context(nc.psum_tensor("PS_T", [128, 1024], BF16))
    PS_MISC = _st.enter_context(nc.psum_tensor("PS_MISC", [128, 512], F32))
    block = _st.enter_context(nc.Block())
    if True:
        nc.const_aps.register(F32, EPS, CONSTS[:, 0:1])
        nc.const_aps.register(F32, 1.0, CONSTS[:, 1:2])
        nc.const_aps.register(F32, 1e-30, CONSTS[:, 2:3])
        nc.const_aps.register(F32, -0.25, CONSTS[:, 3:4])
        try:
            print("SBUF remaining:", nc.sbuf_bytes_remaining)
        except Exception as e:
            print("rem err", e)
        print("base/top:", nc.sbuf_base, nc.sbuf_top)
        WMD = [SCR2, XP]
        XBUF = [X0, X1]
        KA = XP
        RSIG = XP
        XRB = XC
        D1B = XC
        CTIL = EQ

        def rep3(tens, reps, width, inner_step=1, offset_elems=0, outer_step=0):
            a = tens.ap()
            return bass.AP(a.tensor, a.offset + offset_elems,
                           [list(a.ap[0]), [outer_step, reps], [inner_step, width]])

        def ap3(tens, offset_elems, dims):
            a = tens.ap()
            return bass.AP(a.tensor, a.offset + offset_elems, dims)

        @block.sync
        def _(sy):
            if "sy" in _SKIP:
                sy = SkipDma(sy)
            for c in range(min(2, NCH)):
                sy.dma_start(out=ap3(XBUF[c % 2], 0, [[1024, 128], [128, 8], [1, 128]]),
                             in_=xin[c].rearrange("(j p) t -> p j t", p=128)).then_inc(
                                 s_din if c % 2 == 0 else s_din1, 16)
            for i in range(12):
                sy.wait_ge(s_ac, AC.pv(f"mr{i}"))
                sy.dma_start(out=modscr[0:1, i * 512:(i + 1) * 512],
                             in_=STATS[:1, :]).then_inc(s_dsc, 16)
            sy.wait_ge(s_dsc, 16 * 12)
            sy.wait_ge(s_gp, 1)
            sy.dma_start(out=MODCT[0:48, :],
                         in_=modscr.rearrange("o (c d) -> o c d", d=128)[0]).then_inc(s_dsc, 16)
            for c in range(NCH):
                sy.wait_ge(s_ve, VE.v(c, "x3_7"))
                sy.dma_start(out=xout[c].rearrange("(j p) t -> p j t", p=128),
                             in_=ap3(XBUF[c % 2], 0, [[1024, 128], [128, 8], [1, 128]])).then_inc(s_dout, 16)
                if c + 2 < NCH:
                    sy.wait_ge(s_dout, 16 * (c + 1))
                    sy.dma_start(out=ap3(XBUF[c % 2], 0, [[1024, 128], [128, 8], [1, 128]]),
                                 in_=xin[c + 2].rearrange("(j p) t -> p j t", p=128)).then_inc(
                                     s_din if c % 2 == 0 else s_din1, 16)
            sy.wait_ge(s_dout, 16 * NCH)

        @block.gpsimd
        def _(gp):
            gp = Rec(SkipCompute(gp) if "gpc" in _SKIP else gp)
            if "gpd" in _SKIP:
                gp = SkipDma(gp)
            gp.dma_start(out=CROW[:], in_=crow_d).then_inc(s_dsm, 16)
            gp.dma_start(out=ALOG[:], in_=alog_d).then_inc(s_dsm, 16)
            gp.dma_start(out=DPIN[:], in_=dp_d).then_inc(s_dsm, 16)
            gp.dma_start(out=DTB[:], in_=dtb_d).then_inc(s_dsm, 16)
            gp.dma_start(out=CW[:], in_=convw_d).then_inc(s_dsm, 16)
            gp.dma_start(out=CONVB[:], in_=convb_d).then_inc(s_dsm, 16)
            gp.dma_start(out=NORMW[:], in_=normw_d).then_inc(s_dsm, 16)
            gp.dma_start(out=MUR1[:], in_=mur_d).then_inc(s_dsm, 16)
            gp.dma_start(out=MUK1[:], in_=muk_d).then_inc(s_dsm, 16)
            for nb in range(3):
                for k in range(NKD):
                    idx = nb * NKD + k
                    if idx >= 2:
                        pnb, pk = divmod(idx - 2, NKD)
                        gp.wait_ge(s_pe, PE.pv(f"mod{pnb}_{pk}"))
                    gp.dma_start(out=WMD[idx % 2].ap()[:, 0:2048],
                                 in_=wmodb[k, nb]).then_inc(s_dwm0 if idx % 2 == 0 else s_dwm1, 16)
            for k in range(NKD):
                gp.dma_start(out=WIP[:, k * DPROJ:(k + 1) * DPROJ], in_=wipt[k]).then_inc(s_dwip, 16)
            for qq in range(4):
                gp.dma_start(out=ap3(WOP, qq * 4096, [[16 * D, 128], [1024, 4], [1, 1024]]),
                             in_=wopt[qq * 4:(qq + 1) * 4].rearrange("a p f -> p a f")).then_inc(s_dwop, 16)
            for W, dsrc in ((WR, wrt), (WK, wkt), (WV, wvt)):
                for hh in range(2):
                    gp.dma_start(out=ap3(W, hh * 4096, [[NKD * D, 128], [1024, 4], [1, 1024]]),
                                 in_=dsrc[hh * 4:(hh + 1) * 4].rearrange("a p f -> p a f")).then_inc(s_dffn, 16)
            gp.wait_ge(s_dsm, 144)
            gp.dma_start(out=DPFLAT[:1, :], in_=DPIN[:, 0:1]).then_inc(s_dpfl, 16)

            gp.memset(IDENTB[:], 0.0)
            gp.drain()
            gp.affine_select(out=IDENTB[:], in_=IDENTB[:], compare_op=ALU.not_equal,
                             fill=1.0, base=0, pattern=[[-1, 128]], channel_multiplier=1)
            gp.memset(IDENTF[:], 0.0)
            gp.drain()
            gp.affine_select(out=IDENTF[:], in_=IDENTF[:], compare_op=ALU.not_equal,
                             fill=1.0, base=0, pattern=[[-1, 128]], channel_multiplier=1)
            gp.memset(MODCT[:], 0.0)
            gp.memset(MASK01[:], 1.0)
            gp.drain()
            gp.affine_select(out=MASK01[:], in_=MASK01[:], compare_op=ALU.is_ge,
                             fill=0.0, base=0, pattern=[[1, 128]], channel_multiplier=-1)
            gp.memset(MASKNEG[:], 0.0)
            gp.drain()
            gp.affine_select(out=MASKNEG[:], in_=MASKNEG[:], compare_op=ALU.is_ge,
                             fill=-1e30, base=0, pattern=[[1, 128]], channel_multiplier=-1)
            gp.memset(CONSTS[:, 0:1], EPS)
            gp.memset(CONSTS[:, 1:2], 1.0)
            gp.memset(CONSTS[:, 2:3], 1e-30)
            gp.memset(CONSTS[:, 3:4], -0.25)
            gp.memset(ZERO32[:], 0.0)
            gp.memset(CONVTAIL[:], 0.0)
            gp.memset(TAILH2[:], 0.0)
            gp.memset(STATE[:], 0.0)
            gp.memset(ONESB[:], 1.0)
            gp.memset(ONES1F[:], 1.0)
            gp.memset(ONES128[:], 1.0)
            gp.memset(ONES2X[:, :], 1.0)
            gp.drain()
            gp.nop().then_inc(s_gp, 1)

            for c in range(NCH):
                gp.wait_ge(s_ve, VE.v(c, "hilo"))
                if c > 0:
                    gp.wait_ge(s_pe, PE.v(c - 1, "w3b7"))
                gp.dma_start(out=PACK[0:1, :], in_=CUMHI[:, :]).then_inc(s_dfl, 16)
                gp.dma_start(out=PACK[1:2, :], in_=CUMLO[:, :]).then_inc(s_dfl, 16)

        @block.scalar
        def _(ac):
            ac = Rec(SkipCompute(ac) if "ac" in _SKIP else ac)
            ac.wait_ge(s_dsm, 144)
            ac.activation(SILC[:], CROW[:], AF.Silu).then_inc(s_ac, 1)
            for nb in range(3):
                ac.wait_ge(s_pe, PE.pv(f"mod{nb}_{NKD - 1}"))
                for sub in range(4):
                    i = nb * 4 + sub
                    if i > 0:
                        ac.wait_ge(s_dsc, 16 * i)
                    psrc = (PS_BC[0:1, (sub % 2) * 512:(sub % 2) * 512 + 512] if sub < 2
                            else PS_MM[0:1, (sub % 2) * 512:(sub % 2) * 512 + 512])
                    ac.activation(STATS[0:1, :], psrc, AF.Copy)
                    ac.inc(s_ac)
            ac.activation(ACOL[:], ALOG[:], AF.Exp)
            ac.drain()
            ac.activation(ACOL[:], ACOL[:], AF.Copy, scale=-1.0)
            ac.inc(s_ac)
            ac.wait_ge(s_pe, PE.pv("modt"))
            ac.wait_ge(s_gp, 1)
            for gi, srcc in enumerate((16, 40)):
                o = gi * 24
                ac.activation(RSBS[:, o + 0:o + 8], PS_MISC[:, 128 + srcc:128 + srcc + 8], AF.Sign)
                ac.activation(RSBS[:, o + 8:o + 16], PS_MISC[:, 128 + srcc:128 + srcc + 8], AF.Abs)
                ac.drain()
                ac.activation(RSBS[:, o + 16:o + 24], RSBS[:, o + 8:o + 16], AF.Abs, bias=-0.25)
            ac.inc(s_ac)
            ac.wait_ge(s_din, 16)
            ac.activation(SCR2[:, 1024:2048], XBUF[0].ap()[:, :], AF.Square)
            ac.activation(SCR2[:, 0:1024], XBUF[0].ap()[:, :],
                          AF.Copy).then_inc(s_ac, 1)

            for c in range(NCH):
                xb = XBUF[c % 2]
                ac.wait_ge(s_din if c % 2 == 0 else s_din1, 16 * (c // 2 + 1))
                if c == 0:
                    ac.wait_ge(s_pe, PE.pv(f"mod2_{NKD - 1}"))
                    ac.wait_ge(s_dsm, 144)
                else:
                    ac.wait_ge(s_pe, PE.v(c - 1, "stats2"))
                    ac.wait_ge(s_ve, VE.v(c - 1, "ln2apply"))
                ac.wait_ge(s_pe, PE.v(c, "stats"))
                if c == 0:
                    ac.wait_ge(s_dsc, 16 * 12)
                ac.activation(STATS[0:1, 0:256], PS_MISC[0:1, 0:256], AF.Copy,
                              scale=1.0 / 1024.0).then_inc(s_ac, 1)
                ac.wait_ge(s_ve, VE.v(c, "lnrows"))
                ac.activation(STATS[0:1, 256:384], STATS[0:1, 384:512], AF.Ln, bias=EPS)
                ac.drain()
                ac.activation(STATS[0:1, 128:256], STATS[0:1, 256:384], AF.Exp,
                              scale=-0.5).then_inc(s_ac, 1)
                for p in range(2):
                    ac.wait_ge(s_pe, PE.v(c, f"proj{p * 8 + 7}"))
                    for j in range(p * 8, p * 8 + 8):
                        ac.activation(XP[:, j * 131 + 3:j * 131 + 131],
                                      PS_MM[:, mmslot(j):mmslot(j) + 128], AF.Copy)
                    ac.inc(s_ac)
                    if p == 0:
                        ac.wait_ge(s_ve, VE.v(c, "tailcp"))
                    ac.drain()
                    for j in range(p * 8, p * 8 + 8):
                        ac.activation(XC[:, j * 128:(j + 1) * 128],
                                      XP[:, j * 131 + 0:j * 131 + 128],
                                      AF.Copy, scale=CW[:, j * 4 + 0:j * 4 + 1])
                    ac.inc(s_ac)
                ac.wait_ge(s_pe, PE.v(c, f"proj{C_TILE}"))
                ac.activation(BNT[:], PS_MM[:, mmslot(B_TILE):mmslot(B_TILE) + 128], AF.Copy)
                ac.activation(CNT[:], PS_MM[:, mmslot(C_TILE):mmslot(C_TILE) + 128], AF.Copy)
                ac.inc(s_ac)
                ac.wait_ge(s_pe, PE.v(c, f"proj{DT_TILE}"))
                ac.activation(E32[:, :], PS_MM[0:32, mmslot(DT_TILE):mmslot(DT_TILE) + 128],
                              AF.Exp, bias=DTB[:, 0:1]).then_inc(s_ac, 1)
                ac.drain()
                ac.activation(DT32[:, :], E32[:, :], AF.Ln, bias=1.0)
                ac.drain()
                ac.activation(E32[:, :], DT32[:, :], AF.Copy, scale=ACOL[:, 0:1]).then_inc(s_ac, 1)
                for half in range(2):
                    for j in range(half * 8, half * 8 + 8):
                        m = Z_OFF + j
                        ac.wait_ge(s_pe, PE.v(c, f"proj{m}"))
                        ac.activation(SZ[:, j * 128:(j + 1) * 128],
                                      PS_MM[:, mmslot(m):mmslot(m) + 128],
                                      AF.Silu).then_inc(s_ac, 1)
                    for j in range(half * 8, half * 8 + 8):
                        ac.wait_ge(s_ve, VE.v(c, f"tapp{half}"))
                        ac.activation(XC[:, j * 128:(j + 1) * 128], XC[:, j * 128:(j + 1) * 128],
                                      AF.Silu, bias=CONVB[:, j:j + 1]).then_inc(s_ac, 1)
                for o in range(8):
                    b = o % 2
                    if o == 0:
                        ac.wait_ge(s_ve, VE.v(c, "xdt"))
                    if o >= 2:
                        ac.wait_ge(s_ve, VE.v(c, f"ctil{o - 2}"))
                    elif c > 0:
                        ac.wait_ge(s_ve, VE.v(c - 1, f"ctil{o + 6}"))
                    ac.wait_ge(s_pe, PE.v(c, f"bc{o}"))
                    for l in range(4):
                        h = 4 * o + l
                        ac.activation(EQ[:, b * 512 + l * 128:b * 512 + (l + 1) * 128],
                                      PS_BC[:, b * 512 + l * 128:b * 512 + (l + 1) * 128],
                                      AF.Exp, bias=CUMT[:, h:h + 1])
                    ac.inc(s_ac)
                    ac.wait_ge(s_pe, PE.v(c, f"w3b{o}"))
                    ac.wait_ge(s_ve, VE.v(c, f"bsc{o}"))
                    ac.activation(EQ[:, b * 512:(b + 1) * 512],
                                  PS_BC[:, b * 512:(b + 1) * 512], AF.Exp).then_inc(s_ac, 1)
                    ac.activation(W4Q[:, b * 4:b * 4 + 4],
                                  ap3(PS_BC, b * 512 + 127, [[1024, 128], [128, 4]]),
                                  AF.Exp).then_inc(s_ac, 1)
                    if o % 2 == 1:
                        q = o // 2
                        ac.wait_ge(s_pe, PE.v(c, f"y{o}"))
                        ac.activation(SCR2[:, q * 512:(q + 1) * 512], PS_Y[:, :],
                                      AF.Square).then_inc(s_ac, 1)
                ac.wait_ge(s_pe, PE.v(c, "rsred3"))
                ac.activation(STATS[0:1, 0:128], PS_MISC[0:1, 0:128], AF.Ln,
                              scale=1.0 / DI, bias=EPS).then_inc(s_ac, 1)
                ac.wait_ge(s_pe, PE.v(c, "rsbc"))
                ac.activation(RSF[:, :], PS_MISC[:, 128:256], AF.Exp,
                              scale=-0.5).then_inc(s_ac, 1)
                ac.wait_ge(s_ve, VE.v(c, "x2_7"))
                ac.activation(SCR2[:, 1024:2048], xb.ap()[:, :], AF.Square).then_inc(s_ac, 1)
                ac.activation(SCR2[:, 0:1024], xb.ap()[:, :], AF.Copy).then_inc(s_ac, 1)
                ac.wait_ge(s_pe, PE.v(c, "stats2"))
                ac.activation(STATS[0:1, 0:256], PS_MISC[0:1, 0:256], AF.Copy,
                              scale=1.0 / 1024.0).then_inc(s_ac, 1)
                ac.wait_ge(s_ve, VE.v(c, "ln2rows"))
                ac.activation(STATS[0:1, 256:384], STATS[0:1, 384:512], AF.Ln, bias=EPS)
                ac.drain()
                ac.activation(STATS[0:1, 128:256], STATS[0:1, 256:384], AF.Exp,
                              scale=-0.5).then_inc(s_ac, 1)
                if c < NCH - 1:
                    ac.wait_ge(s_din1 if c % 2 == 0 else s_din, 16 * ((c + 1) // 2 + 1))
                    ac.wait_ge(s_ve, VE.v(c, "ln2apply"))
                    ac.wait_ge(s_pe, PE.v(c, "stats2"))
                    ac.activation(SCR2[:, 1024:2048], XBUF[(c + 1) % 2].ap()[:, :],
                                  AF.Square)
                    ac.activation(SCR2[:, 0:1024], XBUF[(c + 1) % 2].ap()[:, :],
                                  AF.Copy).then_inc(s_ac, 1)
                else:
                    ac.activation(ZERO32[0:1, 0:1], ZERO32[0:1, 0:1],
                                  AF.Copy).then_inc(s_ac, 1)
                for m in range(8):
                    ac.wait_ge(s_pe, PE.v(c, f"r{m}"))
                    ac.activation(XP[:, 1024 + m * 128:1024 + (m + 1) * 128],
                                  PS_MM[:, mmslot(m):mmslot(m) + 128],
                                  AF.Sigmoid).then_inc(s_ac, 1)
                for m in range(8):
                    ac.wait_ge(s_pe, PE.v(c, f"k{m}"))
                    ac.activation(XP[:, m * 128:(m + 1) * 128],
                                  PS_MM[:, mmslot(m):mmslot(m) + 128],
                                  AF.Relu).then_inc(s_ac, 1)

        @block.vector
        def _(ve):
            ve = Rec(SkipCompute(ve) if "ve" in _SKIP else ve)
            ve.wait_ge(s_pe, PE.pv("dpbc"))
            ve.tensor_copy(DPB[:], PS_MISC[:, 0:32])
            ve.inc(s_ve)
            ve.drain()
            ve.tensor_copy(DPC[0:64, :], ap3(DPB, 0, [[32, 64], [2, 16]]))
            ve.drain()
            ve.tensor_copy(DPC[64:128, :], ap3(DPB, 64 * 32 + 1, [[32, 64], [2, 16]]))
            ve.inc(s_ve)
            ve.wait_ge(s_pe, PE.pv("modt"))
            ve.wait_ge(s_ac, AC.pv("gclamp"))
            ve.tensor_copy(MODC[:, :], PS_MISC[:, 128:176])
            ve.drain()
            ve.tensor_copy(PARAMS[:, PSA:PSA + 8], MODC[:, 0:8])
            ve.tensor_scalar(PARAMS[:, PCA:PCA + 8], MODC[:, 8:16], 1.0, None, ALU.add)
            ve.tensor_copy(PARAMS[:, PSB:PSB + 8], MODC[:, 24:32])
            ve.tensor_scalar(PARAMS[:, PCB:PCB + 8], MODC[:, 32:40], 1.0, None, ALU.add)
            ve.drain()
            ve.wait_ge(s_ac, AC.pv("gclamp"))
            for gi, dst in ((0, PGA), (1, PGB)):
                o = gi * 24
                ve.tensor_tensor(RSBS[:, o + 8:o + 16], RSBS[:, o + 8:o + 16],
                                 RSBS[:, o + 16:o + 24], ALU.add)
                ve.drain()
                ve.tensor_scalar(RSBS[:, o + 8:o + 16], RSBS[:, o + 8:o + 16],
                                 0.5, 0.125, ALU.mult, ALU.add)
                ve.drain()
                ve.tensor_tensor(PARAMS[:, dst:dst + 8], RSBS[:, o + 0:o + 8],
                                 RSBS[:, o + 8:o + 16], ALU.mult)
            ve.inc(s_ve)
            ve.wait_ge(s_dwop, 64)
            ve.wait_ge(s_dsm, 144)
            for k in range(16):
                ve.tensor_scalar(WOP[:, k * D:(k + 1) * D], WOP[:, k * D:(k + 1) * D],
                                 NORMW[:, k:k + 1], None, ALU.mult)
            ve.inc(s_ve)
            ve.tensor_scalar(MUR1[:], MUR1[:], -1.0, None, ALU.add)
            ve.tensor_scalar(MUK1[:], MUK1[:], -1.0, None, ALU.add)
            ve.inc(s_ve)

            for c in range(NCH):
                xb = XBUF[c % 2]
                if c == 0:
                    ve.wait_ge(s_pe, PE.pv(f"mod2_{NKD - 1}"))
                    ve.wait_ge(s_gp, 1)
                    ve.wait_ge(s_dsm, 144)
                ve.tensor_copy(rep3(XP, 16, 3, 1, 0, 131),
                               CONVTAIL[:, :].rearrange("p (j q) -> p j q", q=3))
                ve.inc(s_ve)
                ve.wait_ge(s_din if c % 2 == 0 else s_din1, 16 * (c // 2 + 1))
                ve.wait_ge(s_ac, AC.v(c, "mqrow"))
                ve.tensor_tensor(STATS[0:1, 256:384], STATS[0:1, 0:128], STATS[0:1, 0:128], ALU.mult)
                ve.drain()
                ve.tensor_tensor(STATS[0:1, 384:512], STATS[0:1, 128:256], STATS[0:1, 256:384],
                                 ALU.subtract)
                ve.inc(s_ve)
                ve.wait_ge(s_pe, PE.v(c, "bcast"))
                ve.tensor_tensor(SCR2[:, 0:1024].rearrange("p (r t) -> p r t", t=128),
                                 xb.ap()[:, :].rearrange("p (r t) -> p r t", t=128),
                                 rep3(PS_MISC, 8, 128, 1, 256), ALU.subtract)
                ve.drain()
                ve.tensor_tensor(SCR2[:, 1024:2048].rearrange("p (r t) -> p r t", t=128),
                                 SCR2[:, 0:1024].rearrange("p (r t) -> p r t", t=128),
                                 rep3(PS_MISC, 8, 128, 1, 384), ALU.mult)
                ve.drain()
                for j in range(8):
                    ve.tensor_scalar(HB[:, j * 128:(j + 1) * 128],
                                     SCR2[:, 1024 + j * 128:1024 + (j + 1) * 128],
                                     PARAMS[:, PCA + j:PCA + j + 1], PARAMS[:, PSA + j:PSA + j + 1],
                                     ALU.mult, ALU.add)
                ve.inc(s_ve)
                for p in range(2):
                    tiles = list(range(p * 8, p * 8 + 8))
                    ve.wait_ge(s_ac, AC.v(c, f"t0{p}"))
                    for qq in range(1, 4):
                        for j in tiles:
                            ve.scalar_tensor_tensor(XC[:, j * 128:(j + 1) * 128],
                                                    XP[:, j * 131 + qq:j * 131 + qq + 128],
                                                    CW[:, j * 4 + qq:j * 4 + qq + 1],
                                                    XC[:, j * 128:(j + 1) * 128],
                                                    ALU.mult, ALU.add)
                        if qq < 3:
                            ve.drain()
                    ve.inc(s_ve)
                ve.wait_ge(s_ac, AC.v(c, "dt2"))
                ve.tensor_tensor_scan(CUM[:, :], E32[:, :], ZERO32[:, :], 0.0, ALU.add, ALU.add)
                ve.inc(s_ve)
                if c > 0:
                    ve.wait_ge(s_dfl, 16 * 2 * c)
                ve.drain()
                ve.tensor_copy(CUMHI[:, :], CUM[:, :])
                ve.drain()
                ve.tensor_tensor(CUMLO[:, :], CUM[:, :], CUMHI[:, :], ALU.subtract)
                ve.inc(s_ve)
                ve.wait_ge(s_pe, PE.v(c, "g"))
                ve.tensor_tensor(GMT[:], PS_ST[:, 0:128], MASK01[:], ALU.mult)
                ve.tensor_copy(BSN[:], PS_T[:, 0:128])
                ve.inc(s_ve)
                ve.wait_ge(s_pe, PE.v(c, "dtt"))
                ve.tensor_copy(DTT[:, :], PS_MISC[:, 128:160])
                ve.tensor_scalar(CUMT[:, :], PS_MISC[:, 160:192], -1.0, None, ALU.mult)
                ve.inc(s_ve)
                ve.drain()
                for b in range(2):
                    ve.wait_ge(s_pe, PE.v(c, f"xt{b + 1}"))
                    ve.tensor_tensor(XT[:, b * 1024:(b + 1) * 1024].rearrange(
                                         "p (h e) -> p h e", e=64),
                                     PS_T[:, :].rearrange("p (h e) -> p h e", e=64),
                                     rep3(DTT, 16, 64, 0, b * 16, 1), ALU.mult)
                    ve.inc(s_ve)
                ve.wait_ge(s_pe, PE.v(c, "xt2"))
                for j in range(16):
                    ve.tensor_scalar(XC[:, j * 128:(j + 1) * 128], XC[:, j * 128:(j + 1) * 128],
                                     DPC[:, j:j + 1], None, ALU.mult)
                ve.inc(s_ve)
                ve.tensor_copy(CONVTAIL[:, :].rearrange("p (j q) -> p j q", q=3),
                               rep3(XP, 16, 3, 1, 128, 131))
                ve.inc(s_ve)
                ve.drain()
                def emit_stev(o):
                    b1 = o % 2
                    ve.wait_ge(s_pe, PE.v(c, f"st{o}"))
                    ve.wait_ge(s_ac, AC.v(c, f"w4a{o}"))
                    for l in range(4):
                        h = 4 * o + l
                        ve.scalar_tensor_tensor(STATE[:, h * 64:(h + 1) * 64],
                                                STATE[:, h * 64:(h + 1) * 64],
                                                W4Q[:, b1 * 4 + l:b1 * 4 + l + 1],
                                                PS_ST[:, b1 * 256 + l * 64:b1 * 256 + (l + 1) * 64],
                                                ALU.mult, ALU.add)
                    ve.inc(s_ve)

                for o in range(8):
                    b = o % 2
                    ve.wait_ge(s_ac, AC.v(c, f"e{o}"))
                    if o >= 2:
                        ve.wait_ge(s_pe, PE.v(c, f"y{o - 2}"))
                    elif c > 0:
                        ve.wait_ge(s_pe, PE.v(c - 1, f"y{o + 6}"))
                    ve.tensor_tensor(ap3(XP, b * 512, [[2096, 128], [128, 4], [1, 128]]),
                                     ap3(EQ, b * 512, [[1024, 128], [128, 4], [1, 128]]),
                                     rep3(GMT, 4, 128), ALU.mult)
                    ve.inc(s_ve)
                    ve.tensor_tensor(ap3(XP, 1024 + b * 512, [[2096, 128], [128, 4], [1, 128]]),
                                     rep3(BSN, 4, 128),
                                     ap3(EQ, b * 512 + 127, [[1024, 128], [128, 4], [0, 128]]),
                                     ALU.mult)
                    ve.inc(s_ve)
                    if o >= 1:
                        emit_stev(o - 1)
                    ve.wait_ge(s_ac, AC.v(c, f"ew3{o}"))
                    ve.tensor_tensor(ap3(HB, b * 512, [[1024, 128], [128, 4], [1, 128]]),
                                     rep3(CNT, 4, 128),
                                     ap3(EQ, b * 512, [[1024, 128], [128, 4], [1, 128]]),
                                     ALU.mult)
                    ve.inc(s_ve)
                    if o % 2 == 1:
                        q = o // 2
                        ve.wait_ge(s_pe, PE.v(c, f"y{o}"))
                        ve.wait_ge(s_ac, AC.v(c, f"ysq{q}"))
                        ve.tensor_tensor(YS[:, q * 512:(q + 1) * 512], PS_Y[:, :],
                                         SZ[:, q * 512:(q + 1) * 512], ALU.mult)
                        ve.inc(s_ve)
                emit_stev(7)
                ve.wait_ge(s_ac, AC.v(c, "rse"))
                ve.drain()
                ve.tensor_tensor(YS[:, :].rearrange("p (j t) -> p j t", t=128),
                                 YS[:, :].rearrange("p (j t) -> p j t", t=128),
                                 rep3(RSF, 16, 128), ALU.mult)
                ve.inc(s_ve)
                for m in range(8):
                    ve.wait_ge(s_pe, PE.v(c, f"op{m}"))
                    ve.scalar_tensor_tensor(xb.ap()[:, m * 128:(m + 1) * 128],
                                            PS_MM[:, mmslot(m):mmslot(m) + 128],
                                            PARAMS[:, PGA + m:PGA + m + 1],
                                            xb.ap()[:, m * 128:(m + 1) * 128], ALU.mult, ALU.add)
                    ve.inc(s_ve)
                ve.wait_ge(s_ac, AC.v(c, "mqrow2"))
                ve.tensor_tensor(STATS[0:1, 256:384], STATS[0:1, 0:128], STATS[0:1, 0:128], ALU.mult)
                ve.drain()
                ve.tensor_tensor(STATS[0:1, 384:512], STATS[0:1, 128:256], STATS[0:1, 256:384],
                                 ALU.subtract)
                ve.inc(s_ve)
                ve.wait_ge(s_pe, PE.v(c, "bcast2"))
                ve.tensor_tensor(SCR2[:, 0:1024].rearrange("p (r t) -> p r t", t=128),
                                 xb.ap()[:, :].rearrange("p (r t) -> p r t", t=128),
                                 rep3(PS_MISC, 8, 128, 1, 256), ALU.subtract)
                ve.drain()
                ve.tensor_tensor(SCR2[:, 1024:2048].rearrange("p (r t) -> p r t", t=128),
                                 SCR2[:, 0:1024].rearrange("p (r t) -> p r t", t=128),
                                 rep3(PS_MISC, 8, 128, 1, 384), ALU.mult)
                ve.drain()
                for j in range(8):
                    ve.tensor_scalar(HB[:, j * 128:(j + 1) * 128],
                                     SCR2[:, 1024 + j * 128:1024 + (j + 1) * 128],
                                     PARAMS[:, PCB + j:PCB + j + 1], PARAMS[:, PSB + j:PSB + j + 1],
                                     ALU.mult, ALU.add)
                ve.inc(s_ve)
                ve.drain()
                ve.tensor_tensor(
                    ap3(XC, 1024 + 1, [[2048, 128], [128, 8], [1, 127]]),
                    ap3(HB, 1, [[1024, 128], [128, 8], [1, 127]]),
                    ap3(HB, 0, [[1024, 128], [128, 8], [1, 127]]), ALU.subtract)
                ve.tensor_tensor(
                    ap3(XC, 1024, [[2048, 128], [128, 8], [1, 1]]),
                    ap3(HB, 0, [[1024, 128], [128, 8], [1, 1]]),
                    ap3(TAILH2, (c % 2) * 8, [[16, 128], [1, 8], [1, 1]]), ALU.subtract)
                ve.tensor_copy(ap3(TAILH2, ((c + 1) % 2) * 8, [[16, 128], [1, 8], [1, 1]]),
                               ap3(HB, 127, [[1024, 128], [128, 8], [1, 1]]))
                ve.inc(s_ve)
                ve.drain()
                for j in range(8):
                    ve.scalar_tensor_tensor(XC[:, j * 128:(j + 1) * 128],
                                            XC[:, 1024 + j * 128:1024 + (j + 1) * 128],
                                            MUR1[:, j:j + 1], HB[:, j * 128:(j + 1) * 128],
                                            ALU.mult, ALU.add)
                ve.inc(s_ve)
                for j in range(8):
                    ve.scalar_tensor_tensor(YS[:, j * 128:(j + 1) * 128],
                                            XC[:, 1024 + j * 128:1024 + (j + 1) * 128],
                                            MUK1[:, j:j + 1], HB[:, j * 128:(j + 1) * 128],
                                            ALU.mult, ALU.add)
                ve.inc(s_ve)
                ve.wait_ge(s_ac, AC.v(c, "rel7"))
                ve.tensor_tensor(XP[:, 0:1024], XP[:, 0:1024], XP[:, 0:1024], ALU.mult)
                ve.inc(s_ve)
                for m in range(8):
                    ve.wait_ge(s_pe, PE.v(c, f"v{m}"))
                    ve.tensor_tensor(YS[:, m * 128:(m + 1) * 128],
                                     PS_MM[:, mmslot(m):mmslot(m) + 128],
                                     XP[:, 1024 + m * 128:1024 + (m + 1) * 128], ALU.mult)
                    ve.inc(s_ve)
                ve.drain()
                for m in range(8):
                    ve.scalar_tensor_tensor(xb.ap()[:, m * 128:(m + 1) * 128],
                                            YS[:, m * 128:(m + 1) * 128],
                                            PARAMS[:, PGB + m:PGB + m + 1],
                                            xb.ap()[:, m * 128:(m + 1) * 128], ALU.mult, ALU.add)
                    ve.inc(s_ve)

        @block.tensor
        def _(pe):
            if "pe" in _SKIP:
                pe = SkipCompute(pe)
            pe.wait_ge(s_ac, AC.pv("silc"))
            for nb in range(3):
                if nb > 0:
                    pe.wait_ge(s_ac, AC.pv(f"mr{4 * nb - 1}"))
                for k in range(NKD):
                    idx = nb * NKD + k
                    pe.wait_ge(s_dwm0 if idx % 2 == 0 else s_dwm1, 16 * (idx // 2 + 1))
                    buf = WMD[idx % 2]
                    for sub in range(4):
                        dst = (PS_BC[0:1, (sub % 2) * 512:(sub % 2) * 512 + 512] if sub < 2
                               else PS_MM[0:1, (sub % 2) * 512:(sub % 2) * 512 + 512])
                        mm = pe.matmul(dst, SILC[:, k:k + 1], buf.ap()[:, sub * 512:(sub + 1) * 512],
                                       start=(k == 0), stop=(k == NKD - 1))
                    mm.then_inc(s_pe, 1)
            pe.wait_ge(s_dpfl, 16)
            pe.wait_ge(s_gp, 1)
            pe.matmul(PS_MISC[:, 0:32], ONESB[:1, :], DPFLAT[:1, :],
                      start=True, stop=True).then_inc(s_pe, 1)
            pe.wait_ge(s_dsc, 16 * 13)
            pe.transpose(PS_MISC[:, 128:256], MODCT[:], IDENTF[:]).then_inc(s_pe, 1)

            for c in range(NCH):
                xb = XBUF[c % 2]
                pe.wait_ge(s_din if c % 2 == 0 else s_din1, 16 * (c // 2 + 1))
                if c == 0:
                    pe.wait_ge(s_ve, VE.pv("mu1"))
                    pe.wait_ge(s_ac, AC.pv("sq0"))
                else:
                    pe.wait_ge(s_ac, AC.v(c - 1, "sqn"))
                if c > 0:
                    pe.wait_ge(s_ve, VE.v(c - 1, "ln2apply"))
                for j in range(8):
                    mm = pe.matmul(PS_MISC[0:1, 0:256], ONES128[:, 0:1],
                                   ap3(SCR2, j * 128, [[2048, 128], [1024, 2], [1, 128]]),
                                   start=(j == 0), stop=(j == 7))
                mm.then_inc(s_pe, 1)
                pe.wait_ge(s_ac, AC.v(c, "rstd"))
                pe.matmul(PS_MISC[:, 256:512], ONES1F[:1, :], STATS[0:1, 0:256],
                          start=True, stop=True).then_inc(s_pe, 1)
                pe.wait_ge(s_ve, VE.v(c, "lnapply"))
                if c == 0:
                    pe.wait_ge(s_dwip, 128)
                def emit_proj(m, idx):
                    if idx == 8:
                        pe.wait_ge(s_ac, AC.v(c, "xpp0"))
                    elif idx == 16:
                        pe.wait_ge(s_ac, AC.v(c, "xpp1"))
                    if m == 34:
                        pe.wait_ge(s_ac, AC.v(c, "bcev"))
                    elif m == 16:
                        pe.wait_ge(s_ac, AC.v(c, "dte"))
                    elif 18 <= m < 32:
                        pe.wait_ge(s_ac, AC.v(c, f"z{m - 18}"))
                    mw = 128 if m != DT_TILE else 32
                    sl = mmslot(m)
                    for k in range(NKD):
                        mm = pe.matmul(PS_MM[0:mw, sl:sl + 128],
                                       WIP[:, k * DPROJ + m * 128:k * DPROJ + m * 128 + mw],
                                       HB[:, k * 128:(k + 1) * 128],
                                       start=(k == 0), stop=(k == NKD - 1))
                    mm.then_inc(s_pe, 1)

                for idx, m in enumerate(PROJ_ORDER[:19]):
                    emit_proj(m, idx)
                pe.wait_ge(s_ac, AC.v(c, "bcev"))
                if c > 0:
                    pe.wait_ge(s_ve, VE.v(c - 1, "stev7"))
                pe.matmul(PS_ST[:, 0:128], BNT[:], CNT[:], start=True, stop=True)
                pe.transpose(PS_T[:, 0:128], BNT[:], IDENTB[:]).then_inc(s_pe, 1)
                for idx, m in enumerate(PROJ_ORDER[19:]):
                    emit_proj(m, idx + 19)
                pe.wait_ge(s_ac, AC.v(c, "dt2"))
                pe.wait_ge(s_ve, VE.v(c, "cum"))
                pe.transpose(PS_MISC[:, 128:160], DT32[:], IDENTF[0:32, 0:32])
                pe.transpose(PS_MISC[:, 160:192], CUM[:], IDENTF[0:32, 0:32]).then_inc(s_pe, 1)
                pe.wait_ge(s_ac, AC.v(c, "xc15"))
                pe.wait_ge(s_ve, VE.v(c, "gmbsn"))
                if c > 0:
                    pe.wait_ge(s_ve, VE.v(c - 1, "xk"))
                for b in range(2):
                    if b == 1:
                        pe.wait_ge(s_ve, VE.v(c, "xtev1"))
                    for j in range(8):
                        jj = b * 8 + j
                        mm = pe.transpose(PS_T[:, j * 128:(j + 1) * 128],
                                          XC[:, jj * 128:(jj + 1) * 128], IDENTB[:])
                    mm.then_inc(s_pe, 1)
                def emit_bc(o):
                    b = o % 2
                    if o >= 2:
                        pe.wait_ge(s_ac, AC.v(c, f"w4a{o - 2}"))
                    elif c > 0:
                        pe.wait_ge(s_ac, AC.v(c - 1, f"w4a{o + 6}"))
                    dst = PS_BC[:, b * 512:(b + 1) * 512]
                    pe.matmul(dst, ONES2X[0:2, :], PACK[0:2, o * 512:(o + 1) * 512],
                              start=True, stop=False)
                    mm = pe.matmul(dst, IDENTB[:],
                                   ap3(MASKNEG, 0, [[128, 128], [0, 4], [1, 128]]),
                                   start=False, stop=True)
                    mm.then_inc(s_pe, 1)

                def emit_w3b(o):
                    b = o % 2
                    pe.wait_ge(s_ac, AC.v(c, f"e{o}"))
                    mm = pe.matmul(PS_BC[:, b * 512:(b + 1) * 512], ONES2X[0:2, :],
                                   PACK[0:2, o * 512:(o + 1) * 512],
                                   start=True, stop=True)
                    mm.then_inc(s_pe, 1)

                pe.wait_ge(s_dfl, 16 * 2 * (c + 1))
                emit_bc(0)
                emit_w3b(0)
                emit_bc(1)
                for o in range(8):
                    b = o % 2
                    if o == 0:
                        pe.wait_ge(s_ve, VE.v(c, "dpxv"))
                    if o >= 2:
                        pe.wait_ge(s_ve, VE.v(c, f"ys{(o - 2) // 2}"))
                        pe.wait_ge(s_ac, AC.v(c, f"ysq{(o - 2) // 2}"))
                    elif c > 0:
                        pe.wait_ge(s_ve, VE.v(c - 1, "ys3"))
                        pe.wait_ge(s_ac, AC.v(c - 1, "ysq3"))
                    pe.wait_ge(s_ve, VE.v(c, f"s{o}"))
                    pe.wait_ge(s_ve, VE.v(c, f"ctil{o}"))
                    for l in range(4):
                        jj, hh = l // 2, l % 2
                        jt = 2 * o + jj
                        h = 4 * o + l
                        ycol = b * 256 + jj * 128
                        reg = PS_Y[hh * 64:(hh + 1) * 64, ycol:ycol + 128]
                        pe.matmul(reg, IDENTB[:, hh * 64:(hh + 1) * 64],
                                  XC[:, jt * 128:(jt + 1) * 128],
                                  start=True, stop=False, tile_position=(0, hh * 64))
                        pe.matmul(reg,
                                  XT[:, jt * 128 + hh * 64:jt * 128 + hh * 64 + 64],
                                  XP[:, b * 512 + l * 128:b * 512 + (l + 1) * 128],
                                  start=False, stop=False, tile_position=(0, hh * 64))
                        mm = pe.matmul(reg, STATE[:, h * 64:(h + 1) * 64],
                                       HB[:, b * 512 + l * 128:b * 512 + (l + 1) * 128],
                                       start=False, stop=True, tile_position=(0, hh * 64))
                    mm.then_inc(s_pe, 1)
                    pe.wait_ge(s_ve, VE.v(c, f"bsc{o}"))
                    if o >= 1:
                        pe.wait_ge(s_ve, VE.v(c, f"stev{o - 1}"))
                    elif c > 0:
                        pe.wait_ge(s_ve, VE.v(c - 1, "stev7"))
                    for l in range(4):
                        jj, hh = l // 2, l % 2
                        jt = 2 * o + jj
                        mm = pe.matmul(PS_ST[:, b * 256 + l * 64:b * 256 + (l + 1) * 64],
                                       XP[:, 1024 + b * 512 + l * 128:1024 + b * 512 + (l + 1) * 128],
                                       XT[:, jt * 128 + hh * 64:jt * 128 + hh * 64 + 64],
                                       start=True, stop=True)
                    mm.then_inc(s_pe, 1)
                    if o <= 6:
                        emit_w3b(o + 1)
                    if o <= 5:
                        emit_bc(o + 2)
                    if o % 2 == 1:
                        q = o // 2
                        pe.wait_ge(s_ac, AC.v(c, f"ysq{q}"))
                        for jj in range(4):
                            mm = pe.matmul(PS_MISC[0:1, 0:128], ONES128[:, 0:1],
                                           SCR2[:, q * 512 + jj * 128:q * 512 + (jj + 1) * 128],
                                           start=(q == 0 and jj == 0), stop=(q == 3 and jj == 3))
                        mm.then_inc(s_pe, 1)
                pe.wait_ge(s_ac, AC.v(c, "rs"))
                pe.matmul(PS_MISC[:, 128:256], ONES1F[:1, :], STATS[0:1, 0:128],
                          start=True, stop=True).then_inc(s_pe, 1)
                pe.wait_ge(s_ve, VE.v(c, "yn"))
                if c == 0:
                    pe.wait_ge(s_ve, VE.pv("wopfold"))
                for m in range(8):
                    if m >= 2:
                        pe.wait_ge(s_ve, VE.v(c, f"x2_{m - 2}"))
                    sl = mmslot(m)
                    for k in range(16):
                        mm = pe.matmul(PS_MM[:, sl:sl + 128],
                                       WOP[:, k * D + m * 128:k * D + (m + 1) * 128],
                                       YS[:, k * 128:(k + 1) * 128], start=(k == 0), stop=(k == 15))
                    mm.then_inc(s_pe, 1)
                pe.wait_ge(s_ac, AC.v(c, "cp2"))
                for j in range(8):
                    mm = pe.matmul(PS_MISC[0:1, 0:256], ONES128[:, 0:1],
                                   ap3(SCR2, j * 128, [[2048, 128], [1024, 2], [1, 128]]),
                                   start=(j == 0), stop=(j == 7))
                mm.then_inc(s_pe, 1)
                pe.wait_ge(s_ac, AC.v(c, "rstd2"))
                pe.matmul(PS_MISC[:, 256:512], ONES1F[:1, :], STATS[0:1, 0:256],
                          start=True, stop=True).then_inc(s_pe, 1)
                pe.wait_ge(s_ve, VE.v(c, "xr"))
                if c == 0:
                    pe.wait_ge(s_dffn, 96)
                for m in range(8):
                    if m >= 2:
                        pe.wait_ge(s_ac, AC.v(c, f"sig{m - 2}"))
                    sl = mmslot(m)
                    for k in range(NKD):
                        mm = pe.matmul(PS_MM[:, sl:sl + 128],
                                       WR[:, k * D + m * 128:k * D + (m + 1) * 128],
                                       XC[:, k * 128:(k + 1) * 128], start=(k == 0), stop=(k == NKD - 1))
                    mm.then_inc(s_pe, 1)
                pe.wait_ge(s_ve, VE.v(c, "xk"))
                for m in range(8):
                    pe.wait_ge(s_ac, AC.v(c, f"sig{m}"))
                    if m >= 2:
                        pe.wait_ge(s_ac, AC.v(c, f"rel{m - 2}"))
                    sl = mmslot(m)
                    for k in range(NKD):
                        mm = pe.matmul(PS_MM[:, sl:sl + 128],
                                       WK[:, k * D + m * 128:k * D + (m + 1) * 128],
                                       YS[:, k * 128:(k + 1) * 128], start=(k == 0), stop=(k == NKD - 1))
                    mm.then_inc(s_pe, 1)
                pe.wait_ge(s_ve, VE.v(c, "ka2"))
                for m in range(8):
                    if m >= 2:
                        pe.wait_ge(s_ve, VE.v(c, f"vsig{m - 2}"))
                    sl = mmslot(m)
                    for k in range(NKD):
                        mm = pe.matmul(PS_MM[:, sl:sl + 128],
                                       WV[:, k * D + m * 128:k * D + (m + 1) * 128],
                                       XP[:, k * 128:(k + 1) * 128], start=(k == 0), stop=(k == NKD - 1))
                    mm.then_inc(s_pe, 1)

    _st.close()
    return nc


def prep_core_inputs(ins, b, NCH):
    x = np.asarray(ins["L0_attn_x"][b], dtype=np.float32)
    feed = {}
    feed["x"] = np.ascontiguousarray(x.reshape(NCH, L, D).transpose(0, 2, 1))
    wip = np.asarray(ins["L0_attn_in_proj_w"], np.float32)
    feed["wipt"] = np.ascontiguousarray(wip.T.reshape(NKD, 128, DPROJ))
    wop = np.asarray(ins["L0_attn_out_proj_w"], np.float32)
    feed["wopt"] = np.ascontiguousarray(wop.T.reshape(16, 128, D))
    for nm, key in (("wrt", "L0_ffn_W_r"), ("wkt", "L0_ffn_W_k"), ("wvt", "L0_ffn_W_v")):
        w = np.asarray(ins[key], np.float32)
        feed[nm] = np.ascontiguousarray(w.T.reshape(NKD, 128, D))
    wm = np.asarray(ins["L0_wmod"], np.float32)
    feed["wmodb"] = np.ascontiguousarray(wm.reshape(NKD, 128, 3, 2048).transpose(0, 2, 1, 3))
    feed["crow"] = np.ascontiguousarray(np.asarray(ins["L0_c"][b], np.float32).reshape(8, 128).T)
    feed["alog"] = np.ascontiguousarray(np.asarray(ins["L0_attn_A_log"], np.float32).reshape(32, 1))
    feed["dpv"] = np.ascontiguousarray(np.asarray(ins["L0_attn_Dp"], np.float32).reshape(32, 1))
    feed["dtb"] = np.ascontiguousarray(np.asarray(ins["L0_attn_dt_bias"], np.float32).reshape(32, 1))
    cw = np.asarray(ins["L0_attn_conv_w"], np.float32).reshape(DI, 4)
    feed["convw"] = np.ascontiguousarray(cw.reshape(16, 128, 4).transpose(1, 0, 2).reshape(128, 64))
    feed["convb"] = np.ascontiguousarray(np.asarray(ins["L0_attn_conv_b"], np.float32).reshape(16, 128).T)
    feed["normw"] = np.ascontiguousarray(np.asarray(ins["L0_attn_norm_w"], np.float32).reshape(16, 128).T)
    feed["mur"] = np.ascontiguousarray(np.asarray(ins["L0_ffn_mu_r"], np.float32).reshape(8, 128).T)
    feed["muk"] = np.ascontiguousarray(np.asarray(ins["L0_ffn_mu_k"], np.float32).reshape(8, 128).T)
    return feed


def out_to_host(o, NCH):
    return np.ascontiguousarray(np.asarray(o).reshape(NCH, D, L).transpose(0, 2, 1)).reshape(NCH * L, D)




_NC_CACHE = {}


def _get_nc(nch):
    if nch not in _NC_CACHE:
        nc = build_nc(nch)
        nc.finalize()
        _NC_CACHE[nch] = nc
    return _NC_CACHE[nch]


def kernel(**inputs):
    from concourse.bass_utils import run_bass_kernel_spmd
    nch = inputs["L0_attn_x"].shape[1] // L
    nc = _get_nc(nch)
    in_maps = [prep_core_inputs(inputs, b, nch) for b in range(8)]
    res = run_bass_kernel_spmd(nc, in_maps, core_ids=list(range(8)))
    outs = []
    for b in range(8):
        o = np.asarray(res.results[b]["out"], dtype=np.float32)
        outs.append(o.reshape(nch, D, L).transpose(0, 2, 1).reshape(nch * L, D))
    return np.ascontiguousarray(np.stack(outs, axis=0))


# version 29 (final)
# speedup vs baseline: 1.0294x; 1.0294x over previous
import numpy as np
import concourse.bass as bass
import concourse.mybir as mybir

F32 = mybir.dt.float32
BF16 = mybir.dt.bfloat16
AF = mybir.ActivationFunctionType
ALU = mybir.AluOpType

D = 1024
DI = 2048
NH = 32
DPROJ = 4384
L = 128
EPS = 1e-5
NKD = 8
NMP = 35
Z_OFF = 16
B_TILE = 32
C_TILE = 33
DT_TILE = 34
PROJ_ORDER = list(range(16)) + [32, 33, 34] + list(range(16, 32))
PSA, PCA, PGA, PSB, PCB, PGB = 0, 8, 16, 24, 32, 40



_OPS = ("tensor_tensor", "tensor_scalar", "tensor_copy", "scalar_tensor_tensor",
        "tensor_tensor_scan", "memset", "activation", "reciprocal", "iota",
        "affine_select", "tensor_reduce", "reduce_sum", "reduce_max", "select",
        "tensor_single_scalar", "copy", "add", "mul", "square", "sign",
        "matmul", "transpose", "nop")


import os
_SKIP = set(filter(None, os.environ.get("MK_SKIP", "").split(",")))


class SkipCompute:
    def __init__(self, eng):
        self._e = eng

    def __getattr__(self, n):
        a = getattr(self._e, n)
        if n in _OPS + ("ldweights",):
            eng = self._e

            def w(*args, **kw):
                return eng.nop()
            return w
        return a


class SkipDma:
    def __init__(self, eng):
        self._e = eng

    def __getattr__(self, n):
        a = getattr(self._e, n)
        if n == "dma_start":
            eng = self._e

            def w(*args, **kw):
                class FakeInst:
                    def then_inc(self, sem, v):
                        return eng.sem_inc(sem, v)
                return FakeInst()
            return w
        return a


class Rec:

    def __init__(self, eng):
        self._e = eng
        self.last = None

    def __getattr__(self, n):
        a = getattr(self._e, n)
        if n in _OPS:
            rec = self

            def w(*args, **kw):
                r = a(*args, **kw)
                rec.last = r
                return r
            return w
        return a

    def inc(self, sem, v=1):
        return self.last.then_inc(sem, v)


class Sched:
    def __init__(self, prologue_tags, chunk_tags):
        self.pt = {t: i + 1 for i, t in enumerate(prologue_tags)}
        self.P = len(prologue_tags)
        self.ct = {t: i + 1 for i, t in enumerate(chunk_tags)}
        self.N = len(chunk_tags)

    def pv(self, tag):
        return self.pt[tag]

    def v(self, c, tag):
        return self.P + c * self.N + self.ct[tag]


def build_nc(NCH):
    nc = bass.Bass("TRN2", target_bir_lowering=False)

    xin = nc.dram_tensor("x", [NCH, D, L], F32, kind="ExternalInput").ap()
    xout = nc.dram_tensor("out", [NCH, D, L], F32, kind="ExternalOutput").ap()
    wipt = nc.dram_tensor("wipt", [NKD, 128, DPROJ], F32, kind="ExternalInput").ap()
    wopt = nc.dram_tensor("wopt", [16, 128, D], F32, kind="ExternalInput").ap()
    wrt = nc.dram_tensor("wrt", [NKD, 128, D], F32, kind="ExternalInput").ap()
    wkt = nc.dram_tensor("wkt", [NKD, 128, D], F32, kind="ExternalInput").ap()
    wvt = nc.dram_tensor("wvt", [NKD, 128, D], F32, kind="ExternalInput").ap()
    wmodb = nc.dram_tensor("wmodb", [NKD, 3, 128, 2048], F32, kind="ExternalInput").ap()
    crow_d = nc.dram_tensor("crow", [128, 8], F32, kind="ExternalInput").ap()
    alog_d = nc.dram_tensor("alog", [32, 1], F32, kind="ExternalInput").ap()
    dp_d = nc.dram_tensor("dpv", [32, 1], F32, kind="ExternalInput").ap()
    dtb_d = nc.dram_tensor("dtb", [32, 1], F32, kind="ExternalInput").ap()
    convw_d = nc.dram_tensor("convw", [128, 64], F32, kind="ExternalInput").ap()
    convb_d = nc.dram_tensor("convb", [128, 16], F32, kind="ExternalInput").ap()
    normw_d = nc.dram_tensor("normw", [128, 16], F32, kind="ExternalInput").ap()
    mur_d = nc.dram_tensor("mur", [128, 8], F32, kind="ExternalInput").ap()
    muk_d = nc.dram_tensor("muk", [128, 8], F32, kind="ExternalInput").ap()
    modscr = nc.dram_tensor("modscr", [1, 6144], F32, kind="Internal").ap()

    pe_p = [f"mod{nb}_{k}" for nb in range(3) for k in range(NKD)] + ["dpbc", "modt"]
    pe_c = (["stats", "bcast"] + [f"proj{m}" for m in PROJ_ORDER[:19]] + ["g"]
            + [f"proj{m}" for m in PROJ_ORDER[19:]] + ["dtt", "xt1", "xt2"]
            + ["bc0", "w3b0", "bc1"]
            + sum([[f"y{o}", f"st{o}"]
                   + ([f"w3b{o + 1}"] if o <= 6 else [])
                   + ([f"bc{o + 2}"] if o <= 5 else [])
                   + ([f"rsred{o // 2}"] if o % 2 == 1 else [])
                   for o in range(8)], [])
            + ["rsbc"] + [f"op{m}" for m in range(8)] + ["stats2", "bcast2"]
            + [f"r{m}" for m in range(8)] + [f"k{m}" for m in range(8)]
            + [f"v{m}" for m in range(8)])
    PE = Sched(pe_p, pe_c)

    ve_p = ["dpb", "dpc", "params", "wopfold", "mu1"]
    ve_c = (["tailcp", "lnrows", "lnapply"]
            + ["tapp0", "tapp1", "cum", "hilo", "gmbsn", "xdt", "xtev1", "xtev2", "dpxv",
               "newtail"]
            + sum([[f"s{o}", f"bsc{o}"]
                   + ([f"stev{o - 1}"] if o >= 1 else [])
                   + [f"ctil{o}"]
                   + ([f"ys{o // 2}"] if o % 2 == 1 else [])
                   for o in range(8)], [])
            + ["stev7", "yn"] + [f"x2_{m}" for m in range(8)]
            + ["ln2rows", "ln2apply", "d1", "xr", "xk", "ka2"]
            + [f"vsig{m}" for m in range(8)] + [f"x3_{m}" for m in range(8)])
    VE = Sched(ve_p, ve_c)

    ac_p = ["silc"] + [f"mr{i}" for i in range(12)] + ["acol", "gclamp", "sq0"]
    ac_c = (["mqrow", "rstd", "xpp0", "xpp1", "bcev", "dte", "dt2"]
            + [f"z{j}" for j in range(8)] + [f"xc{j}" for j in range(8)]
            + [f"z{j}" for j in range(8, 16)] + [f"xc{j}" for j in range(8, 16)]
            + sum([[f"e{o}", f"ew3{o}", f"w4a{o}"] + ([f"ysq{o // 2}"] if o % 2 == 1 else [])
                   for o in range(8)], [])
            + ["rs", "rse", "sq2", "cp2", "mqrow2", "rstd2", "sqn"]
            + [f"sig{m}" for m in range(8)] + [f"rel{m}" for m in range(8)])
    AC = Sched(ac_p, ac_c)

    NSMALL = 9

    def dgp_wmod(nb, k):
        return NSMALL + nb * NKD + k + 1

    DGP_WIP0 = NSMALL + 24
    DGP_WOP0 = DGP_WIP0 + NKD
    DGP_FFN0 = DGP_WOP0 + 4
    DGP_DPF = DGP_FFN0 + 6 + 1
    DGP_P = DGP_DPF
    DGP_C = 2

    def dgp_v(i):
        return 16 * i

    def dgp_cv(c, i):
        return 16 * (DGP_P + c * DGP_C + i)

    def proj_evac(m):
        if m < 16:
            return ("ve", f"xp{m}")
        if m < 32:
            return ("ac", f"z{m - 16}")
        if m in (32, 33):
            return ("ac", "bcev")
        return ("ac", "dte")

    def mmslot(m):
        return (m % 2) * 512 + ((m // 2) % 4) * 128

    import contextlib
    _st = contextlib.ExitStack()
    s_pe = _st.enter_context(nc.semaphore("s_pe"))
    s_ve = _st.enter_context(nc.semaphore("s_ve"))
    s_ac = _st.enter_context(nc.semaphore("s_ac"))
    s_gp = _st.enter_context(nc.semaphore("s_gp"))
    s_din = _st.enter_context(nc.semaphore("s_din"))
    s_dout = _st.enter_context(nc.semaphore("s_dout"))
    s_dsc = _st.enter_context(nc.semaphore("s_dsc"))
    s_dgp = _st.enter_context(nc.semaphore("s_dgp"))
    s_dsm = _st.enter_context(nc.semaphore("s_dsm"))
    s_dwm0 = _st.enter_context(nc.semaphore("s_dwm0"))
    s_dwm1 = _st.enter_context(nc.semaphore("s_dwm1"))
    s_dwip = _st.enter_context(nc.semaphore("s_dwip"))
    s_dwop = _st.enter_context(nc.semaphore("s_dwop"))
    s_dffn = _st.enter_context(nc.semaphore("s_dffn"))
    s_dpfl = _st.enter_context(nc.semaphore("s_dpfl"))
    s_dfl = _st.enter_context(nc.semaphore("s_dfl"))
    s_din1 = _st.enter_context(nc.semaphore("s_din1"))
    WIP = _st.enter_context(nc.sbuf_tensor("WIP", [128, NKD * DPROJ], BF16))
    WOP = _st.enter_context(nc.sbuf_tensor("WOP", [128, 16 * D], BF16))
    WR = _st.enter_context(nc.sbuf_tensor("WR", [128, NKD * D], BF16))
    WK = _st.enter_context(nc.sbuf_tensor("WK", [128, NKD * D], BF16))
    WV = _st.enter_context(nc.sbuf_tensor("WV", [128, NKD * D], BF16))
    CROW = _st.enter_context(nc.sbuf_tensor("CROW", [128, 8], F32))
    SILC = _st.enter_context(nc.sbuf_tensor("SILC", [128, 8], BF16))
    MODC = _st.enter_context(nc.sbuf_tensor("MODC", [128, 48], F32))
    MODCT = _st.enter_context(nc.sbuf_tensor("MODCT", [128, 128], F32))
    IDENTF = _st.enter_context(nc.sbuf_tensor("IDENTF", [128, 128], F32))
    PARAMS = _st.enter_context(nc.sbuf_tensor("PARAMS", [128, 48], F32))
    ALOG = _st.enter_context(nc.sbuf_tensor("ALOG", [32, 1], F32))
    ACOL = _st.enter_context(nc.sbuf_tensor("ACOL", [32, 1], F32))
    DTB = _st.enter_context(nc.sbuf_tensor("DTB", [32, 1], F32))
    DPIN = _st.enter_context(nc.sbuf_tensor("DPIN", [32, 1], F32))
    DPFLAT = _st.enter_context(nc.sbuf_tensor("DPFLAT", [1, 32], BF16))
    DPB = _st.enter_context(nc.sbuf_tensor("DPB", [128, 32], F32))
    DPC = _st.enter_context(nc.sbuf_tensor("DPC", [128, 16], F32))
    CW = _st.enter_context(nc.sbuf_tensor("CW", [128, 64], F32))
    CONVB = _st.enter_context(nc.sbuf_tensor("CONVB", [128, 16], F32))
    NORMW = _st.enter_context(nc.sbuf_tensor("NORMW", [128, 16], F32))
    MUR1 = _st.enter_context(nc.sbuf_tensor("MUR1", [128, 8], F32))
    MUK1 = _st.enter_context(nc.sbuf_tensor("MUK1", [128, 8], F32))
    ONESB = _st.enter_context(nc.sbuf_tensor("ONESB", [1, 128], BF16))
    ONES1F = _st.enter_context(nc.sbuf_tensor("ONES1F", [1, 128], F32))
    ONES128 = _st.enter_context(nc.sbuf_tensor("ONES128", [128, 1], BF16))
    IDENTB = _st.enter_context(nc.sbuf_tensor("IDENTB", [128, 128], BF16))
    MASK01 = _st.enter_context(nc.sbuf_tensor("MASK01", [128, 128], BF16))
    MASKNEG = _st.enter_context(nc.sbuf_tensor("MASKNEG", [128, 128], BF16))
    ZERO32 = _st.enter_context(nc.sbuf_tensor("ZERO32", [32, 128], F32))
    W4Q = _st.enter_context(nc.sbuf_tensor("W4Q", [128, 8], F32))
    CONSTS = _st.enter_context(nc.sbuf_tensor("CONSTS", [128, 4], F32))
    PACK = _st.enter_context(nc.sbuf_tensor("PACK", [2, 4096], BF16))
    ONES2X = _st.enter_context(nc.sbuf_tensor("ONES2X", [2, 128], BF16))
    CONVTAIL = _st.enter_context(nc.sbuf_tensor("CONVTAIL", [128, 48], BF16))
    TAILH2 = _st.enter_context(nc.sbuf_tensor("TAILH2", [128, 16], BF16))
    RSF = _st.enter_context(nc.sbuf_tensor("RSF", [128, 128], BF16))
    STATE = _st.enter_context(nc.sbuf_tensor("STATE", [128, 2048], BF16))
    X0 = _st.enter_context(nc.sbuf_tensor("X0", [128, 1024], F32))
    X1 = _st.enter_context(nc.sbuf_tensor("X1", [128, 1024], F32))
    HB = _st.enter_context(nc.sbuf_tensor("HB", [128, 1024], BF16))
    SCR2 = _st.enter_context(nc.sbuf_tensor("SCR2", [128, 2048], BF16))
    XP = _st.enter_context(nc.sbuf_tensor("XP", [128, 2096], BF16))
    XC = _st.enter_context(nc.sbuf_tensor("XC", [128, 2048], BF16))
    XT = _st.enter_context(nc.sbuf_tensor("XT", [128, 2048], BF16))
    SZ = _st.enter_context(nc.sbuf_tensor("SZ", [128, 2048], BF16))
    BNT = _st.enter_context(nc.sbuf_tensor("BNT", [128, 128], BF16))
    CNT = _st.enter_context(nc.sbuf_tensor("CNT", [128, 128], BF16))
    BSN = _st.enter_context(nc.sbuf_tensor("BSN", [128, 128], BF16))
    GMT = _st.enter_context(nc.sbuf_tensor("GMT", [128, 128], BF16))
    EQ = _st.enter_context(nc.sbuf_tensor("EQ", [128, 1024], BF16))
    YS = _st.enter_context(nc.sbuf_tensor("YS", [128, 2048], BF16))
    RSBS = _st.enter_context(nc.sbuf_tensor("RSBS", [128, 128], F32))
    STATS = _st.enter_context(nc.sbuf_tensor("STATS", [1, 512], F32))
    E32 = _st.enter_context(nc.sbuf_tensor("E32", [32, 128], F32))
    DT32 = _st.enter_context(nc.sbuf_tensor("DT32", [32, 128], F32))
    CUM = _st.enter_context(nc.sbuf_tensor("CUM", [32, 128], F32))
    CUMHI = _st.enter_context(nc.sbuf_tensor("CUMHI", [32, 128], BF16))
    CUMLO = _st.enter_context(nc.sbuf_tensor("CUMLO", [32, 128], BF16))
    DTT = _st.enter_context(nc.sbuf_tensor("DTT", [128, 32], BF16))
    CUMT = _st.enter_context(nc.sbuf_tensor("CUMT", [128, 32], F32))
    PS_Y = _st.enter_context(nc.psum_tensor("PS_Y", [128, 512], F32))
    PS_BC = _st.enter_context(nc.psum_tensor("PS_BC", [128, 1024], F32))
    PS_MM = _st.enter_context(nc.psum_tensor("PS_MM", [128, 1024], F32))
    PS_ST = _st.enter_context(nc.psum_tensor("PS_ST", [128, 512], F32))
    PS_T = _st.enter_context(nc.psum_tensor("PS_T", [128, 1024], BF16))
    PS_MISC = _st.enter_context(nc.psum_tensor("PS_MISC", [128, 512], F32))
    block = _st.enter_context(nc.Block())
    if True:
        nc.const_aps.register(F32, EPS, CONSTS[:, 0:1])
        nc.const_aps.register(F32, 1.0, CONSTS[:, 1:2])
        nc.const_aps.register(F32, 1e-30, CONSTS[:, 2:3])
        nc.const_aps.register(F32, -0.25, CONSTS[:, 3:4])
        try:
            print("SBUF remaining:", nc.sbuf_bytes_remaining)
        except Exception as e:
            print("rem err", e)
        print("base/top:", nc.sbuf_base, nc.sbuf_top)
        WMD = [SCR2, XP]
        XBUF = [X0, X1]
        KA = XP
        RSIG = XP
        XRB = XC
        D1B = XC
        CTIL = EQ

        def rep3(tens, reps, width, inner_step=1, offset_elems=0, outer_step=0):
            a = tens.ap()
            return bass.AP(a.tensor, a.offset + offset_elems,
                           [list(a.ap[0]), [outer_step, reps], [inner_step, width]])

        def ap3(tens, offset_elems, dims):
            a = tens.ap()
            return bass.AP(a.tensor, a.offset + offset_elems, dims)

        @block.sync
        def _(sy):
            if "sy" in _SKIP:
                sy = SkipDma(sy)
            for c in range(min(2, NCH)):
                sy.dma_start(out=ap3(XBUF[c % 2], 0, [[1024, 128], [128, 8], [1, 128]]),
                             in_=xin[c].rearrange("(j p) t -> p j t", p=128)).then_inc(
                                 s_din if c % 2 == 0 else s_din1, 16)
            for i in range(12):
                sy.wait_ge(s_ac, AC.pv(f"mr{i}"))
                sy.dma_start(out=modscr[0:1, i * 512:(i + 1) * 512],
                             in_=STATS[:1, :]).then_inc(s_dsc, 16)
            sy.wait_ge(s_dsc, 16 * 12)
            sy.wait_ge(s_gp, 1)
            sy.dma_start(out=MODCT[0:48, :],
                         in_=modscr.rearrange("o (c d) -> o c d", d=128)[0]).then_inc(s_dsc, 16)
            for c in range(NCH):
                sy.wait_ge(s_ve, VE.v(c, "x3_7"))
                sy.dma_start(out=xout[c].rearrange("(j p) t -> p j t", p=128),
                             in_=ap3(XBUF[c % 2], 0, [[1024, 128], [128, 8], [1, 128]])).then_inc(s_dout, 16)
                if c + 2 < NCH:
                    sy.wait_ge(s_dout, 16 * (c + 1))
                    sy.dma_start(out=ap3(XBUF[c % 2], 0, [[1024, 128], [128, 8], [1, 128]]),
                                 in_=xin[c + 2].rearrange("(j p) t -> p j t", p=128)).then_inc(
                                     s_din if c % 2 == 0 else s_din1, 16)
            sy.wait_ge(s_dout, 16 * NCH)

        @block.gpsimd
        def _(gp):
            gp = Rec(SkipCompute(gp) if "gpc" in _SKIP else gp)
            if "gpd" in _SKIP:
                gp = SkipDma(gp)
            gp.dma_start(out=CROW[:], in_=crow_d).then_inc(s_dsm, 16)
            gp.dma_start(out=ALOG[:], in_=alog_d).then_inc(s_dsm, 16)
            gp.dma_start(out=DPIN[:], in_=dp_d).then_inc(s_dsm, 16)
            gp.dma_start(out=DTB[:], in_=dtb_d).then_inc(s_dsm, 16)
            gp.dma_start(out=CW[:], in_=convw_d).then_inc(s_dsm, 16)
            gp.dma_start(out=CONVB[:], in_=convb_d).then_inc(s_dsm, 16)
            gp.dma_start(out=NORMW[:], in_=normw_d).then_inc(s_dsm, 16)
            gp.dma_start(out=MUR1[:], in_=mur_d).then_inc(s_dsm, 16)
            gp.dma_start(out=MUK1[:], in_=muk_d).then_inc(s_dsm, 16)
            for nb in range(3):
                for k in range(NKD):
                    idx = nb * NKD + k
                    if idx >= 2:
                        pnb, pk = divmod(idx - 2, NKD)
                        gp.wait_ge(s_pe, PE.pv(f"mod{pnb}_{pk}"))
                    gp.dma_start(out=WMD[idx % 2].ap()[:, 0:2048],
                                 in_=wmodb[k, nb]).then_inc(s_dwm0 if idx % 2 == 0 else s_dwm1, 16)
            for k in range(NKD):
                gp.dma_start(out=WIP[:, k * DPROJ:(k + 1) * DPROJ], in_=wipt[k]).then_inc(s_dwip, 16)
            for qq in range(4):
                gp.dma_start(out=ap3(WOP, qq * 4096, [[16 * D, 128], [1024, 4], [1, 1024]]),
                             in_=wopt[qq * 4:(qq + 1) * 4].rearrange("a p f -> p a f")).then_inc(s_dwop, 16)
            for W, dsrc in ((WR, wrt), (WK, wkt), (WV, wvt)):
                for hh in range(2):
                    gp.dma_start(out=ap3(W, hh * 4096, [[NKD * D, 128], [1024, 4], [1, 1024]]),
                                 in_=dsrc[hh * 4:(hh + 1) * 4].rearrange("a p f -> p a f")).then_inc(s_dffn, 16)
            gp.wait_ge(s_dsm, 144)
            gp.dma_start(out=DPFLAT[:1, :], in_=DPIN[:, 0:1]).then_inc(s_dpfl, 16)

            gp.memset(IDENTB[:], 0.0)
            gp.drain()
            gp.affine_select(out=IDENTB[:], in_=IDENTB[:], compare_op=ALU.not_equal,
                             fill=1.0, base=0, pattern=[[-1, 128]], channel_multiplier=1)
            gp.memset(IDENTF[:], 0.0)
            gp.drain()
            gp.affine_select(out=IDENTF[:], in_=IDENTF[:], compare_op=ALU.not_equal,
                             fill=1.0, base=0, pattern=[[-1, 128]], channel_multiplier=1)
            gp.memset(MODCT[:], 0.0)
            gp.memset(MASK01[:], 1.0)
            gp.drain()
            gp.affine_select(out=MASK01[:], in_=MASK01[:], compare_op=ALU.is_ge,
                             fill=0.0, base=0, pattern=[[1, 128]], channel_multiplier=-1)
            gp.memset(MASKNEG[:], 0.0)
            gp.drain()
            gp.affine_select(out=MASKNEG[:], in_=MASKNEG[:], compare_op=ALU.is_ge,
                             fill=-1e30, base=0, pattern=[[1, 128]], channel_multiplier=-1)
            gp.memset(CONSTS[:, 0:1], EPS)
            gp.memset(CONSTS[:, 1:2], 1.0)
            gp.memset(CONSTS[:, 2:3], 1e-30)
            gp.memset(CONSTS[:, 3:4], -0.25)
            gp.memset(ZERO32[:], 0.0)
            gp.memset(CONVTAIL[:], 0.0)
            gp.memset(TAILH2[:], 0.0)
            gp.memset(STATE[:], 0.0)
            gp.memset(ONESB[:], 1.0)
            gp.memset(ONES1F[:], 1.0)
            gp.memset(ONES128[:], 1.0)
            gp.memset(ONES2X[:, :], 1.0)
            gp.drain()
            gp.nop().then_inc(s_gp, 1)

            for c in range(NCH):
                gp.wait_ge(s_ve, VE.v(c, "hilo"))
                if c > 0:
                    gp.wait_ge(s_pe, PE.v(c - 1, "w3b7"))
                gp.dma_start(out=PACK[0:1, :], in_=CUMHI[:, :]).then_inc(s_dfl, 16)
                gp.dma_start(out=PACK[1:2, :], in_=CUMLO[:, :]).then_inc(s_dfl, 16)

        @block.scalar
        def _(ac):
            ac = Rec(SkipCompute(ac) if "ac" in _SKIP else ac)
            ac.wait_ge(s_dsm, 144)
            ac.activation(SILC[:], CROW[:], AF.Silu).then_inc(s_ac, 1)
            for nb in range(3):
                ac.wait_ge(s_pe, PE.pv(f"mod{nb}_{NKD - 1}"))
                for sub in range(4):
                    i = nb * 4 + sub
                    if i > 0:
                        ac.wait_ge(s_dsc, 16 * i)
                    psrc = (PS_BC[0:1, (sub % 2) * 512:(sub % 2) * 512 + 512] if sub < 2
                            else PS_MM[0:1, (sub % 2) * 512:(sub % 2) * 512 + 512])
                    ac.activation(STATS[0:1, :], psrc, AF.Copy)
                    ac.inc(s_ac)
            ac.activation(ACOL[:], ALOG[:], AF.Exp)
            ac.drain()
            ac.activation(ACOL[:], ACOL[:], AF.Copy, scale=-1.0)
            ac.inc(s_ac)
            ac.wait_ge(s_pe, PE.pv("modt"))
            ac.wait_ge(s_gp, 1)
            for gi, srcc in enumerate((16, 40)):
                o = gi * 24
                ac.activation(RSBS[:, o + 0:o + 8], PS_MISC[:, 128 + srcc:128 + srcc + 8], AF.Sign)
                ac.activation(RSBS[:, o + 8:o + 16], PS_MISC[:, 128 + srcc:128 + srcc + 8], AF.Abs)
                ac.drain()
                ac.activation(RSBS[:, o + 16:o + 24], RSBS[:, o + 8:o + 16], AF.Abs, bias=-0.25)
            ac.inc(s_ac)
            ac.wait_ge(s_din, 16)
            ac.activation(SCR2[:, 1024:2048], XBUF[0].ap()[:, :], AF.Square)
            ac.activation(SCR2[:, 0:1024], XBUF[0].ap()[:, :],
                          AF.Copy).then_inc(s_ac, 1)

            for c in range(NCH):
                xb = XBUF[c % 2]
                ac.wait_ge(s_din if c % 2 == 0 else s_din1, 16 * (c // 2 + 1))
                if c == 0:
                    ac.wait_ge(s_pe, PE.pv(f"mod2_{NKD - 1}"))
                    ac.wait_ge(s_dsm, 144)
                else:
                    ac.wait_ge(s_pe, PE.v(c - 1, "stats2"))
                    ac.wait_ge(s_ve, VE.v(c - 1, "ln2apply"))
                ac.wait_ge(s_pe, PE.v(c, "stats"))
                if c == 0:
                    ac.wait_ge(s_dsc, 16 * 12)
                ac.activation(STATS[0:1, 0:256], PS_MISC[0:1, 0:256], AF.Copy,
                              scale=1.0 / 1024.0).then_inc(s_ac, 1)
                ac.wait_ge(s_ve, VE.v(c, "lnrows"))
                ac.activation(STATS[0:1, 256:384], STATS[0:1, 384:512], AF.Ln, bias=EPS)
                ac.drain()
                ac.activation(STATS[0:1, 128:256], STATS[0:1, 256:384], AF.Exp,
                              scale=-0.5).then_inc(s_ac, 1)
                for p in range(2):
                    ac.wait_ge(s_pe, PE.v(c, f"proj{p * 8 + 7}"))
                    for j in range(p * 8, p * 8 + 8):
                        ac.activation(XP[:, j * 131 + 3:j * 131 + 131],
                                      PS_MM[:, mmslot(j):mmslot(j) + 128], AF.Copy)
                    ac.inc(s_ac)
                ac.wait_ge(s_pe, PE.v(c, f"proj{C_TILE}"))
                ac.activation(BNT[:], PS_MM[:, mmslot(B_TILE):mmslot(B_TILE) + 128], AF.Copy)
                ac.activation(CNT[:], PS_MM[:, mmslot(C_TILE):mmslot(C_TILE) + 128], AF.Copy)
                ac.inc(s_ac)
                ac.wait_ge(s_pe, PE.v(c, f"proj{DT_TILE}"))
                ac.activation(E32[:, :], PS_MM[0:32, mmslot(DT_TILE):mmslot(DT_TILE) + 128],
                              AF.Exp, bias=DTB[:, 0:1]).then_inc(s_ac, 1)
                ac.drain()
                ac.activation(DT32[:, :], E32[:, :], AF.Ln, bias=1.0)
                ac.drain()
                ac.activation(E32[:, :], DT32[:, :], AF.Copy, scale=ACOL[:, 0:1]).then_inc(s_ac, 1)
                for half in range(2):
                    for j in range(half * 8, half * 8 + 8):
                        m = Z_OFF + j
                        ac.wait_ge(s_pe, PE.v(c, f"proj{m}"))
                        ac.activation(SZ[:, j * 128:(j + 1) * 128],
                                      PS_MM[:, mmslot(m):mmslot(m) + 128],
                                      AF.Silu).then_inc(s_ac, 1)
                    for j in range(half * 8, half * 8 + 8):
                        ac.wait_ge(s_ve, VE.v(c, f"tapp{half}"))
                        ac.activation(XC[:, j * 128:(j + 1) * 128], XC[:, j * 128:(j + 1) * 128],
                                      AF.Silu, bias=CONVB[:, j:j + 1]).then_inc(s_ac, 1)
                for o in range(8):
                    b = o % 2
                    if o == 0:
                        ac.wait_ge(s_ve, VE.v(c, "xdt"))
                    if o >= 2:
                        ac.wait_ge(s_ve, VE.v(c, f"ctil{o - 2}"))
                    elif c > 0:
                        ac.wait_ge(s_ve, VE.v(c - 1, f"ctil{o + 6}"))
                    ac.wait_ge(s_pe, PE.v(c, f"bc{o}"))
                    for l in range(4):
                        h = 4 * o + l
                        ac.activation(EQ[:, b * 512 + l * 128:b * 512 + (l + 1) * 128],
                                      PS_BC[:, b * 512 + l * 128:b * 512 + (l + 1) * 128],
                                      AF.Exp, bias=CUMT[:, h:h + 1])
                    ac.inc(s_ac)
                    ac.wait_ge(s_pe, PE.v(c, f"w3b{o}"))
                    ac.wait_ge(s_ve, VE.v(c, f"bsc{o}"))
                    ac.activation(EQ[:, b * 512:(b + 1) * 512],
                                  PS_BC[:, b * 512:(b + 1) * 512], AF.Exp).then_inc(s_ac, 1)
                    ac.activation(W4Q[:, b * 4:b * 4 + 4],
                                  ap3(PS_BC, b * 512 + 127, [[1024, 128], [128, 4]]),
                                  AF.Exp).then_inc(s_ac, 1)
                    if o % 2 == 1:
                        q = o // 2
                        ac.wait_ge(s_pe, PE.v(c, f"y{o}"))
                        ac.activation(SCR2[:, q * 512:(q + 1) * 512], PS_Y[:, :],
                                      AF.Square).then_inc(s_ac, 1)
                ac.wait_ge(s_pe, PE.v(c, "rsred3"))
                ac.activation(STATS[0:1, 0:128], PS_MISC[0:1, 0:128], AF.Ln,
                              scale=1.0 / DI, bias=EPS).then_inc(s_ac, 1)
                ac.wait_ge(s_pe, PE.v(c, "rsbc"))
                ac.activation(RSF[:, :], PS_MISC[:, 128:256], AF.Exp,
                              scale=-0.5).then_inc(s_ac, 1)
                ac.wait_ge(s_ve, VE.v(c, "x2_3"))
                ac.activation(SCR2[:, 1024:1536], xb.ap()[:, 0:512], AF.Square)
                ac.activation(SCR2[:, 0:512], xb.ap()[:, 0:512],
                              AF.Copy).then_inc(s_ac, 1)
                ac.wait_ge(s_ve, VE.v(c, "x2_7"))
                ac.activation(SCR2[:, 1536:2048], xb.ap()[:, 512:1024], AF.Square)
                ac.activation(SCR2[:, 512:1024], xb.ap()[:, 512:1024],
                              AF.Copy).then_inc(s_ac, 1)
                ac.wait_ge(s_pe, PE.v(c, "stats2"))
                ac.activation(STATS[0:1, 0:256], PS_MISC[0:1, 0:256], AF.Copy,
                              scale=1.0 / 1024.0).then_inc(s_ac, 1)
                ac.wait_ge(s_ve, VE.v(c, "ln2rows"))
                ac.activation(STATS[0:1, 256:384], STATS[0:1, 384:512], AF.Ln, bias=EPS)
                ac.drain()
                ac.activation(STATS[0:1, 128:256], STATS[0:1, 256:384], AF.Exp,
                              scale=-0.5).then_inc(s_ac, 1)
                if c < NCH - 1:
                    ac.wait_ge(s_din1 if c % 2 == 0 else s_din, 16 * ((c + 1) // 2 + 1))
                    ac.wait_ge(s_ve, VE.v(c, "ln2apply"))
                    ac.wait_ge(s_pe, PE.v(c, "stats2"))
                    ac.activation(SCR2[:, 1024:2048], XBUF[(c + 1) % 2].ap()[:, :],
                                  AF.Square)
                    ac.activation(SCR2[:, 0:1024], XBUF[(c + 1) % 2].ap()[:, :],
                                  AF.Copy).then_inc(s_ac, 1)
                else:
                    ac.activation(ZERO32[0:1, 0:1], ZERO32[0:1, 0:1],
                                  AF.Copy).then_inc(s_ac, 1)
                for m in range(8):
                    ac.wait_ge(s_pe, PE.v(c, f"r{m}"))
                    ac.activation(XP[:, 1024 + m * 128:1024 + (m + 1) * 128],
                                  PS_MM[:, mmslot(m):mmslot(m) + 128],
                                  AF.Sigmoid).then_inc(s_ac, 1)
                for m in range(8):
                    ac.wait_ge(s_pe, PE.v(c, f"k{m}"))
                    ac.activation(XP[:, m * 128:(m + 1) * 128],
                                  PS_MM[:, mmslot(m):mmslot(m) + 128],
                                  AF.Relu).then_inc(s_ac, 1)

        @block.vector
        def _(ve):
            ve = Rec(SkipCompute(ve) if "ve" in _SKIP else ve)
            ve.wait_ge(s_pe, PE.pv("dpbc"))
            ve.tensor_copy(DPB[:], PS_MISC[:, 0:32])
            ve.inc(s_ve)
            ve.drain()
            ve.tensor_copy(DPC[0:64, :], ap3(DPB, 0, [[32, 64], [2, 16]]))
            ve.drain()
            ve.tensor_copy(DPC[64:128, :], ap3(DPB, 64 * 32 + 1, [[32, 64], [2, 16]]))
            ve.inc(s_ve)
            ve.wait_ge(s_pe, PE.pv("modt"))
            ve.wait_ge(s_ac, AC.pv("gclamp"))
            ve.tensor_copy(MODC[:, :], PS_MISC[:, 128:176])
            ve.drain()
            ve.tensor_copy(PARAMS[:, PSA:PSA + 8], MODC[:, 0:8])
            ve.tensor_scalar(PARAMS[:, PCA:PCA + 8], MODC[:, 8:16], 1.0, None, ALU.add)
            ve.tensor_copy(PARAMS[:, PSB:PSB + 8], MODC[:, 24:32])
            ve.tensor_scalar(PARAMS[:, PCB:PCB + 8], MODC[:, 32:40], 1.0, None, ALU.add)
            ve.drain()
            ve.wait_ge(s_ac, AC.pv("gclamp"))
            for gi, dst in ((0, PGA), (1, PGB)):
                o = gi * 24
                ve.tensor_tensor(RSBS[:, o + 8:o + 16], RSBS[:, o + 8:o + 16],
                                 RSBS[:, o + 16:o + 24], ALU.add)
                ve.drain()
                ve.tensor_scalar(RSBS[:, o + 8:o + 16], RSBS[:, o + 8:o + 16],
                                 0.5, 0.125, ALU.mult, ALU.add)
                ve.drain()
                ve.tensor_tensor(PARAMS[:, dst:dst + 8], RSBS[:, o + 0:o + 8],
                                 RSBS[:, o + 8:o + 16], ALU.mult)
            ve.inc(s_ve)
            ve.wait_ge(s_dwop, 64)
            ve.wait_ge(s_dsm, 144)
            for k in range(16):
                ve.tensor_scalar(WOP[:, k * D:(k + 1) * D], WOP[:, k * D:(k + 1) * D],
                                 NORMW[:, k:k + 1], None, ALU.mult)
            ve.inc(s_ve)
            ve.tensor_scalar(MUR1[:], MUR1[:], -1.0, None, ALU.add)
            ve.tensor_scalar(MUK1[:], MUK1[:], -1.0, None, ALU.add)
            ve.inc(s_ve)

            for c in range(NCH):
                xb = XBUF[c % 2]
                if c == 0:
                    ve.wait_ge(s_pe, PE.pv(f"mod2_{NKD - 1}"))
                    ve.wait_ge(s_gp, 1)
                    ve.wait_ge(s_dsm, 144)
                ve.tensor_copy(rep3(XP, 16, 3, 1, 0, 131),
                               CONVTAIL[:, :].rearrange("p (j q) -> p j q", q=3))
                ve.inc(s_ve)
                ve.wait_ge(s_din if c % 2 == 0 else s_din1, 16 * (c // 2 + 1))
                ve.wait_ge(s_ac, AC.v(c, "mqrow"))
                ve.tensor_tensor(STATS[0:1, 256:384], STATS[0:1, 0:128], STATS[0:1, 0:128], ALU.mult)
                ve.drain()
                ve.tensor_tensor(STATS[0:1, 384:512], STATS[0:1, 128:256], STATS[0:1, 256:384],
                                 ALU.subtract)
                ve.inc(s_ve)
                ve.wait_ge(s_pe, PE.v(c, "bcast"))
                ve.tensor_tensor(SCR2[:, 0:1024].rearrange("p (r t) -> p r t", t=128),
                                 xb.ap()[:, :].rearrange("p (r t) -> p r t", t=128),
                                 rep3(PS_MISC, 8, 128, 1, 256), ALU.subtract)
                ve.drain()
                ve.tensor_tensor(SCR2[:, 1024:2048].rearrange("p (r t) -> p r t", t=128),
                                 SCR2[:, 0:1024].rearrange("p (r t) -> p r t", t=128),
                                 rep3(PS_MISC, 8, 128, 1, 384), ALU.mult)
                ve.drain()
                for j in range(8):
                    ve.tensor_scalar(HB[:, j * 128:(j + 1) * 128],
                                     SCR2[:, 1024 + j * 128:1024 + (j + 1) * 128],
                                     PARAMS[:, PCA + j:PCA + j + 1], PARAMS[:, PSA + j:PSA + j + 1],
                                     ALU.mult, ALU.add)
                ve.inc(s_ve)
                for p in range(2):
                    tiles = list(range(p * 8, p * 8 + 8))
                    ve.wait_ge(s_ac, AC.v(c, f"xpp{p}"))
                    for qq in range(4):
                        for j in tiles:
                            if qq == 0:
                                ve.tensor_scalar(XC[:, j * 128:(j + 1) * 128],
                                                 XP[:, j * 131 + 0:j * 131 + 128],
                                                 CW[:, j * 4 + 0:j * 4 + 1], None, ALU.mult)
                            else:
                                ve.scalar_tensor_tensor(XC[:, j * 128:(j + 1) * 128],
                                                        XP[:, j * 131 + qq:j * 131 + qq + 128],
                                                        CW[:, j * 4 + qq:j * 4 + qq + 1],
                                                        XC[:, j * 128:(j + 1) * 128],
                                                        ALU.mult, ALU.add)
                        if qq < 3:
                            ve.drain()
                    ve.inc(s_ve)
                ve.wait_ge(s_ac, AC.v(c, "dt2"))
                ve.tensor_tensor_scan(CUM[:, :], E32[:, :], ZERO32[:, :], 0.0, ALU.add, ALU.add)
                ve.inc(s_ve)
                if c > 0:
                    ve.wait_ge(s_dfl, 16 * 2 * c)
                ve.drain()
                ve.tensor_copy(CUMHI[:, :], CUM[:, :])
                ve.drain()
                ve.tensor_tensor(CUMLO[:, :], CUM[:, :], CUMHI[:, :], ALU.subtract)
                ve.inc(s_ve)
                ve.wait_ge(s_pe, PE.v(c, "g"))
                ve.tensor_tensor(GMT[:], PS_ST[:, 0:128], MASK01[:], ALU.mult)
                ve.tensor_copy(BSN[:], PS_T[:, 0:128])
                ve.inc(s_ve)
                ve.wait_ge(s_pe, PE.v(c, "dtt"))
                ve.tensor_copy(DTT[:, :], PS_MISC[:, 128:160])
                ve.tensor_scalar(CUMT[:, :], PS_MISC[:, 160:192], -1.0, None, ALU.mult)
                ve.inc(s_ve)
                ve.drain()
                for b in range(2):
                    ve.wait_ge(s_pe, PE.v(c, f"xt{b + 1}"))
                    ve.tensor_tensor(XT[:, b * 1024:(b + 1) * 1024].rearrange(
                                         "p (h e) -> p h e", e=64),
                                     PS_T[:, :].rearrange("p (h e) -> p h e", e=64),
                                     rep3(DTT, 16, 64, 0, b * 16, 1), ALU.mult)
                    ve.inc(s_ve)
                ve.wait_ge(s_pe, PE.v(c, "xt2"))
                for j in range(16):
                    ve.tensor_scalar(XC[:, j * 128:(j + 1) * 128], XC[:, j * 128:(j + 1) * 128],
                                     DPC[:, j:j + 1], None, ALU.mult)
                ve.inc(s_ve)
                ve.tensor_copy(CONVTAIL[:, :].rearrange("p (j q) -> p j q", q=3),
                               rep3(XP, 16, 3, 1, 128, 131))
                ve.inc(s_ve)
                ve.drain()
                def emit_stev(o):
                    b1 = o % 2
                    ve.wait_ge(s_pe, PE.v(c, f"st{o}"))
                    ve.wait_ge(s_ac, AC.v(c, f"w4a{o}"))
                    for l in range(4):
                        h = 4 * o + l
                        ve.scalar_tensor_tensor(STATE[:, h * 64:(h + 1) * 64],
                                                STATE[:, h * 64:(h + 1) * 64],
                                                W4Q[:, b1 * 4 + l:b1 * 4 + l + 1],
                                                PS_ST[:, b1 * 256 + l * 64:b1 * 256 + (l + 1) * 64],
                                                ALU.mult, ALU.add)
                    ve.inc(s_ve)

                for o in range(8):
                    b = o % 2
                    ve.wait_ge(s_ac, AC.v(c, f"e{o}"))
                    if o >= 2:
                        ve.wait_ge(s_pe, PE.v(c, f"y{o - 2}"))
                    elif c > 0:
                        ve.wait_ge(s_pe, PE.v(c - 1, f"y{o + 6}"))
                    ve.tensor_tensor(ap3(XP, b * 512, [[2096, 128], [128, 4], [1, 128]]),
                                     ap3(EQ, b * 512, [[1024, 128], [128, 4], [1, 128]]),
                                     rep3(GMT, 4, 128), ALU.mult)
                    ve.inc(s_ve)
                    ve.tensor_tensor(ap3(XP, 1024 + b * 512, [[2096, 128], [128, 4], [1, 128]]),
                                     rep3(BSN, 4, 128),
                                     ap3(EQ, b * 512 + 127, [[1024, 128], [128, 4], [0, 128]]),
                                     ALU.mult)
                    ve.inc(s_ve)
                    if o >= 1:
                        emit_stev(o - 1)
                    ve.wait_ge(s_ac, AC.v(c, f"ew3{o}"))
                    ve.tensor_tensor(ap3(HB, b * 512, [[1024, 128], [128, 4], [1, 128]]),
                                     rep3(CNT, 4, 128),
                                     ap3(EQ, b * 512, [[1024, 128], [128, 4], [1, 128]]),
                                     ALU.mult)
                    ve.inc(s_ve)
                    if o % 2 == 1:
                        q = o // 2
                        ve.wait_ge(s_pe, PE.v(c, f"y{o}"))
                        ve.wait_ge(s_ac, AC.v(c, f"ysq{q}"))
                        ve.tensor_tensor(YS[:, q * 512:(q + 1) * 512], PS_Y[:, :],
                                         SZ[:, q * 512:(q + 1) * 512], ALU.mult)
                        ve.inc(s_ve)
                emit_stev(7)
                ve.wait_ge(s_ac, AC.v(c, "rse"))
                ve.drain()
                ve.tensor_tensor(YS[:, :].rearrange("p (j t) -> p j t", t=128),
                                 YS[:, :].rearrange("p (j t) -> p j t", t=128),
                                 rep3(RSF, 16, 128), ALU.mult)
                ve.inc(s_ve)
                for m in range(8):
                    ve.wait_ge(s_pe, PE.v(c, f"op{m}"))
                    ve.scalar_tensor_tensor(xb.ap()[:, m * 128:(m + 1) * 128],
                                            PS_MM[:, mmslot(m):mmslot(m) + 128],
                                            PARAMS[:, PGA + m:PGA + m + 1],
                                            xb.ap()[:, m * 128:(m + 1) * 128], ALU.mult, ALU.add)
                    ve.inc(s_ve)
                ve.wait_ge(s_ac, AC.v(c, "mqrow2"))
                ve.tensor_tensor(STATS[0:1, 256:384], STATS[0:1, 0:128], STATS[0:1, 0:128], ALU.mult)
                ve.drain()
                ve.tensor_tensor(STATS[0:1, 384:512], STATS[0:1, 128:256], STATS[0:1, 256:384],
                                 ALU.subtract)
                ve.inc(s_ve)
                ve.wait_ge(s_pe, PE.v(c, "bcast2"))
                ve.tensor_tensor(SCR2[:, 0:1024].rearrange("p (r t) -> p r t", t=128),
                                 xb.ap()[:, :].rearrange("p (r t) -> p r t", t=128),
                                 rep3(PS_MISC, 8, 128, 1, 256), ALU.subtract)
                ve.drain()
                ve.tensor_tensor(SCR2[:, 1024:2048].rearrange("p (r t) -> p r t", t=128),
                                 SCR2[:, 0:1024].rearrange("p (r t) -> p r t", t=128),
                                 rep3(PS_MISC, 8, 128, 1, 384), ALU.mult)
                ve.drain()
                for j in range(8):
                    ve.tensor_scalar(HB[:, j * 128:(j + 1) * 128],
                                     SCR2[:, 1024 + j * 128:1024 + (j + 1) * 128],
                                     PARAMS[:, PCB + j:PCB + j + 1], PARAMS[:, PSB + j:PSB + j + 1],
                                     ALU.mult, ALU.add)
                ve.inc(s_ve)
                ve.drain()
                ve.tensor_tensor(
                    ap3(XC, 1024 + 1, [[2048, 128], [128, 8], [1, 127]]),
                    ap3(HB, 1, [[1024, 128], [128, 8], [1, 127]]),
                    ap3(HB, 0, [[1024, 128], [128, 8], [1, 127]]), ALU.subtract)
                ve.tensor_tensor(
                    ap3(XC, 1024, [[2048, 128], [128, 8], [1, 1]]),
                    ap3(HB, 0, [[1024, 128], [128, 8], [1, 1]]),
                    ap3(TAILH2, (c % 2) * 8, [[16, 128], [1, 8], [1, 1]]), ALU.subtract)
                ve.tensor_copy(ap3(TAILH2, ((c + 1) % 2) * 8, [[16, 128], [1, 8], [1, 1]]),
                               ap3(HB, 127, [[1024, 128], [128, 8], [1, 1]]))
                ve.inc(s_ve)
                ve.drain()
                for j in range(8):
                    ve.scalar_tensor_tensor(XC[:, j * 128:(j + 1) * 128],
                                            XC[:, 1024 + j * 128:1024 + (j + 1) * 128],
                                            MUR1[:, j:j + 1], HB[:, j * 128:(j + 1) * 128],
                                            ALU.mult, ALU.add)
                ve.inc(s_ve)
                for j in range(8):
                    ve.scalar_tensor_tensor(YS[:, j * 128:(j + 1) * 128],
                                            XC[:, 1024 + j * 128:1024 + (j + 1) * 128],
                                            MUK1[:, j:j + 1], HB[:, j * 128:(j + 1) * 128],
                                            ALU.mult, ALU.add)
                ve.inc(s_ve)
                ve.wait_ge(s_ac, AC.v(c, "rel7"))
                ve.tensor_tensor(XP[:, 0:1024], XP[:, 0:1024], XP[:, 0:1024], ALU.mult)
                ve.inc(s_ve)
                for m in range(8):
                    ve.wait_ge(s_pe, PE.v(c, f"v{m}"))
                    ve.tensor_tensor(YS[:, m * 128:(m + 1) * 128],
                                     PS_MM[:, mmslot(m):mmslot(m) + 128],
                                     XP[:, 1024 + m * 128:1024 + (m + 1) * 128], ALU.mult)
                    ve.inc(s_ve)
                ve.drain()
                for m in range(8):
                    ve.scalar_tensor_tensor(xb.ap()[:, m * 128:(m + 1) * 128],
                                            YS[:, m * 128:(m + 1) * 128],
                                            PARAMS[:, PGB + m:PGB + m + 1],
                                            xb.ap()[:, m * 128:(m + 1) * 128], ALU.mult, ALU.add)
                    ve.inc(s_ve)

        @block.tensor
        def _(pe):
            if "pe" in _SKIP:
                pe = SkipCompute(pe)
            pe.wait_ge(s_ac, AC.pv("silc"))
            for nb in range(3):
                if nb > 0:
                    pe.wait_ge(s_ac, AC.pv(f"mr{4 * nb - 1}"))
                for k in range(NKD):
                    idx = nb * NKD + k
                    pe.wait_ge(s_dwm0 if idx % 2 == 0 else s_dwm1, 16 * (idx // 2 + 1))
                    buf = WMD[idx % 2]
                    for sub in range(4):
                        dst = (PS_BC[0:1, (sub % 2) * 512:(sub % 2) * 512 + 512] if sub < 2
                               else PS_MM[0:1, (sub % 2) * 512:(sub % 2) * 512 + 512])
                        mm = pe.matmul(dst, SILC[:, k:k + 1], buf.ap()[:, sub * 512:(sub + 1) * 512],
                                       start=(k == 0), stop=(k == NKD - 1))
                    mm.then_inc(s_pe, 1)
            pe.wait_ge(s_dpfl, 16)
            pe.wait_ge(s_gp, 1)
            pe.matmul(PS_MISC[:, 0:32], ONESB[:1, :], DPFLAT[:1, :],
                      start=True, stop=True).then_inc(s_pe, 1)
            pe.wait_ge(s_dsc, 16 * 13)
            pe.transpose(PS_MISC[:, 128:256], MODCT[:], IDENTF[:]).then_inc(s_pe, 1)

            for c in range(NCH):
                xb = XBUF[c % 2]
                pe.wait_ge(s_din if c % 2 == 0 else s_din1, 16 * (c // 2 + 1))
                if c == 0:
                    pe.wait_ge(s_ve, VE.pv("mu1"))
                    pe.wait_ge(s_ac, AC.pv("sq0"))
                else:
                    pe.wait_ge(s_ac, AC.v(c - 1, "sqn"))
                if c > 0:
                    pe.wait_ge(s_ve, VE.v(c - 1, "ln2apply"))
                for j in range(8):
                    mm = pe.matmul(PS_MISC[0:1, 0:256], ONES128[:, 0:1],
                                   ap3(SCR2, j * 128, [[2048, 128], [1024, 2], [1, 128]]),
                                   start=(j == 0), stop=(j == 7))
                mm.then_inc(s_pe, 1)
                pe.wait_ge(s_ac, AC.v(c, "rstd"))
                pe.matmul(PS_MISC[:, 256:512], ONES1F[:1, :], STATS[0:1, 0:256],
                          start=True, stop=True).then_inc(s_pe, 1)
                pe.wait_ge(s_ve, VE.v(c, "lnapply"))
                if c == 0:
                    pe.wait_ge(s_dwip, 128)
                def emit_proj(m, idx):
                    if idx == 8:
                        pe.wait_ge(s_ac, AC.v(c, "xpp0"))
                    elif idx == 16:
                        pe.wait_ge(s_ac, AC.v(c, "xpp1"))
                    if m == 34:
                        pe.wait_ge(s_ac, AC.v(c, "bcev"))
                    elif m == 16:
                        pe.wait_ge(s_ac, AC.v(c, "dte"))
                    elif 18 <= m < 32:
                        pe.wait_ge(s_ac, AC.v(c, f"z{m - 18}"))
                    mw = 128 if m != DT_TILE else 32
                    sl = mmslot(m)
                    for k in range(NKD):
                        mm = pe.matmul(PS_MM[0:mw, sl:sl + 128],
                                       WIP[:, k * DPROJ + m * 128:k * DPROJ + m * 128 + mw],
                                       HB[:, k * 128:(k + 1) * 128],
                                       start=(k == 0), stop=(k == NKD - 1))
                    mm.then_inc(s_pe, 1)

                for idx, m in enumerate(PROJ_ORDER[:19]):
                    emit_proj(m, idx)
                pe.wait_ge(s_ac, AC.v(c, "bcev"))
                if c > 0:
                    pe.wait_ge(s_ve, VE.v(c - 1, "stev7"))
                pe.matmul(PS_ST[:, 0:128], BNT[:], CNT[:], start=True, stop=True)
                pe.transpose(PS_T[:, 0:128], BNT[:], IDENTB[:]).then_inc(s_pe, 1)
                for idx, m in enumerate(PROJ_ORDER[19:]):
                    emit_proj(m, idx + 19)
                pe.wait_ge(s_ac, AC.v(c, "dt2"))
                pe.wait_ge(s_ve, VE.v(c, "cum"))
                pe.transpose(PS_MISC[:, 128:160], DT32[:], IDENTF[0:32, 0:32])
                pe.transpose(PS_MISC[:, 160:192], CUM[:], IDENTF[0:32, 0:32]).then_inc(s_pe, 1)
                pe.wait_ge(s_ac, AC.v(c, "xc15"))
                pe.wait_ge(s_ve, VE.v(c, "gmbsn"))
                if c > 0:
                    pe.wait_ge(s_ve, VE.v(c - 1, "xk"))
                for b in range(2):
                    if b == 1:
                        pe.wait_ge(s_ve, VE.v(c, "xtev1"))
                    for j in range(8):
                        jj = b * 8 + j
                        mm = pe.transpose(PS_T[:, j * 128:(j + 1) * 128],
                                          XC[:, jj * 128:(jj + 1) * 128], IDENTB[:])
                    mm.then_inc(s_pe, 1)
                def emit_bc(o):
                    b = o % 2
                    if o >= 2:
                        pe.wait_ge(s_ac, AC.v(c, f"w4a{o - 2}"))
                    elif c > 0:
                        pe.wait_ge(s_ac, AC.v(c - 1, f"w4a{o + 6}"))
                    dst = PS_BC[:, b * 512:(b + 1) * 512]
                    pe.matmul(dst, ONES2X[0:2, :], PACK[0:2, o * 512:(o + 1) * 512],
                              start=True, stop=False)
                    mm = pe.matmul(dst, IDENTB[:],
                                   ap3(MASKNEG, 0, [[128, 128], [0, 4], [1, 128]]),
                                   start=False, stop=True)
                    mm.then_inc(s_pe, 1)

                def emit_w3b(o):
                    b = o % 2
                    pe.wait_ge(s_ac, AC.v(c, f"e{o}"))
                    mm = pe.matmul(PS_BC[:, b * 512:(b + 1) * 512], ONES2X[0:2, :],
                                   PACK[0:2, o * 512:(o + 1) * 512],
                                   start=True, stop=True)
                    mm.then_inc(s_pe, 1)

                pe.wait_ge(s_dfl, 16 * 2 * (c + 1))
                emit_bc(0)
                emit_w3b(0)
                emit_bc(1)
                for o in range(8):
                    b = o % 2
                    if o == 0:
                        pe.wait_ge(s_ve, VE.v(c, "dpxv"))
                    if o >= 2:
                        pe.wait_ge(s_ve, VE.v(c, f"ys{(o - 2) // 2}"))
                        pe.wait_ge(s_ac, AC.v(c, f"ysq{(o - 2) // 2}"))
                    elif c > 0:
                        pe.wait_ge(s_ve, VE.v(c - 1, "ys3"))
                        pe.wait_ge(s_ac, AC.v(c - 1, "ysq3"))
                    pe.wait_ge(s_ve, VE.v(c, f"s{o}"))
                    pe.wait_ge(s_ve, VE.v(c, f"ctil{o}"))
                    for l in range(4):
                        jj, hh = l // 2, l % 2
                        jt = 2 * o + jj
                        h = 4 * o + l
                        ycol = b * 256 + jj * 128
                        reg = PS_Y[hh * 64:(hh + 1) * 64, ycol:ycol + 128]
                        pe.matmul(reg, IDENTB[:, hh * 64:(hh + 1) * 64],
                                  XC[:, jt * 128:(jt + 1) * 128],
                                  start=True, stop=False, tile_position=(0, hh * 64))
                        pe.matmul(reg,
                                  XT[:, jt * 128 + hh * 64:jt * 128 + hh * 64 + 64],
                                  XP[:, b * 512 + l * 128:b * 512 + (l + 1) * 128],
                                  start=False, stop=False, tile_position=(0, hh * 64))
                        mm = pe.matmul(reg, STATE[:, h * 64:(h + 1) * 64],
                                       HB[:, b * 512 + l * 128:b * 512 + (l + 1) * 128],
                                       start=False, stop=True, tile_position=(0, hh * 64))
                    mm.then_inc(s_pe, 1)
                    pe.wait_ge(s_ve, VE.v(c, f"bsc{o}"))
                    if o >= 1:
                        pe.wait_ge(s_ve, VE.v(c, f"stev{o - 1}"))
                    elif c > 0:
                        pe.wait_ge(s_ve, VE.v(c - 1, "stev7"))
                    for l in range(4):
                        jj, hh = l // 2, l % 2
                        jt = 2 * o + jj
                        mm = pe.matmul(PS_ST[:, b * 256 + l * 64:b * 256 + (l + 1) * 64],
                                       XP[:, 1024 + b * 512 + l * 128:1024 + b * 512 + (l + 1) * 128],
                                       XT[:, jt * 128 + hh * 64:jt * 128 + hh * 64 + 64],
                                       start=True, stop=True)
                    mm.then_inc(s_pe, 1)
                    if o <= 6:
                        emit_w3b(o + 1)
                    if o <= 5:
                        emit_bc(o + 2)
                    if o % 2 == 1:
                        q = o // 2
                        pe.wait_ge(s_ac, AC.v(c, f"ysq{q}"))
                        for jj in range(4):
                            mm = pe.matmul(PS_MISC[0:1, 0:128], ONES128[:, 0:1],
                                           SCR2[:, q * 512 + jj * 128:q * 512 + (jj + 1) * 128],
                                           start=(q == 0 and jj == 0), stop=(q == 3 and jj == 3))
                        mm.then_inc(s_pe, 1)
                pe.wait_ge(s_ac, AC.v(c, "rs"))
                pe.matmul(PS_MISC[:, 128:256], ONES1F[:1, :], STATS[0:1, 0:128],
                          start=True, stop=True).then_inc(s_pe, 1)
                pe.wait_ge(s_ve, VE.v(c, "yn"))
                if c == 0:
                    pe.wait_ge(s_ve, VE.pv("wopfold"))
                for m in range(8):
                    if m >= 2:
                        pe.wait_ge(s_ve, VE.v(c, f"x2_{m - 2}"))
                    sl = mmslot(m)
                    for k in range(16):
                        mm = pe.matmul(PS_MM[:, sl:sl + 128],
                                       WOP[:, k * D + m * 128:k * D + (m + 1) * 128],
                                       YS[:, k * 128:(k + 1) * 128], start=(k == 0), stop=(k == 15))
                    mm.then_inc(s_pe, 1)
                pe.wait_ge(s_ac, AC.v(c, "cp2"))
                for j in range(8):
                    mm = pe.matmul(PS_MISC[0:1, 0:256], ONES128[:, 0:1],
                                   ap3(SCR2, j * 128, [[2048, 128], [1024, 2], [1, 128]]),
                                   start=(j == 0), stop=(j == 7))
                mm.then_inc(s_pe, 1)
                pe.wait_ge(s_ac, AC.v(c, "rstd2"))
                pe.matmul(PS_MISC[:, 256:512], ONES1F[:1, :], STATS[0:1, 0:256],
                          start=True, stop=True).then_inc(s_pe, 1)
                pe.wait_ge(s_ve, VE.v(c, "xr"))
                if c == 0:
                    pe.wait_ge(s_dffn, 96)
                for m in range(8):
                    if m >= 2:
                        pe.wait_ge(s_ac, AC.v(c, f"sig{m - 2}"))
                    sl = mmslot(m)
                    for k in range(NKD):
                        mm = pe.matmul(PS_MM[:, sl:sl + 128],
                                       WR[:, k * D + m * 128:k * D + (m + 1) * 128],
                                       XC[:, k * 128:(k + 1) * 128], start=(k == 0), stop=(k == NKD - 1))
                    mm.then_inc(s_pe, 1)
                pe.wait_ge(s_ve, VE.v(c, "xk"))
                for m in range(8):
                    pe.wait_ge(s_ac, AC.v(c, f"sig{m}"))
                    if m >= 2:
                        pe.wait_ge(s_ac, AC.v(c, f"rel{m - 2}"))
                    sl = mmslot(m)
                    for k in range(NKD):
                        mm = pe.matmul(PS_MM[:, sl:sl + 128],
                                       WK[:, k * D + m * 128:k * D + (m + 1) * 128],
                                       YS[:, k * 128:(k + 1) * 128], start=(k == 0), stop=(k == NKD - 1))
                    mm.then_inc(s_pe, 1)
                pe.wait_ge(s_ve, VE.v(c, "ka2"))
                for m in range(8):
                    if m >= 2:
                        pe.wait_ge(s_ve, VE.v(c, f"vsig{m - 2}"))
                    sl = mmslot(m)
                    for k in range(NKD):
                        mm = pe.matmul(PS_MM[:, sl:sl + 128],
                                       WV[:, k * D + m * 128:k * D + (m + 1) * 128],
                                       XP[:, k * 128:(k + 1) * 128], start=(k == 0), stop=(k == NKD - 1))
                    mm.then_inc(s_pe, 1)

    _st.close()
    return nc


def prep_core_inputs(ins, b, NCH):
    x = np.asarray(ins["L0_attn_x"][b], dtype=np.float32)
    feed = {}
    feed["x"] = np.ascontiguousarray(x.reshape(NCH, L, D).transpose(0, 2, 1))
    wip = np.asarray(ins["L0_attn_in_proj_w"], np.float32)
    feed["wipt"] = np.ascontiguousarray(wip.T.reshape(NKD, 128, DPROJ))
    wop = np.asarray(ins["L0_attn_out_proj_w"], np.float32)
    feed["wopt"] = np.ascontiguousarray(wop.T.reshape(16, 128, D))
    for nm, key in (("wrt", "L0_ffn_W_r"), ("wkt", "L0_ffn_W_k"), ("wvt", "L0_ffn_W_v")):
        w = np.asarray(ins[key], np.float32)
        feed[nm] = np.ascontiguousarray(w.T.reshape(NKD, 128, D))
    wm = np.asarray(ins["L0_wmod"], np.float32)
    feed["wmodb"] = np.ascontiguousarray(wm.reshape(NKD, 128, 3, 2048).transpose(0, 2, 1, 3))
    feed["crow"] = np.ascontiguousarray(np.asarray(ins["L0_c"][b], np.float32).reshape(8, 128).T)
    feed["alog"] = np.ascontiguousarray(np.asarray(ins["L0_attn_A_log"], np.float32).reshape(32, 1))
    feed["dpv"] = np.ascontiguousarray(np.asarray(ins["L0_attn_Dp"], np.float32).reshape(32, 1))
    feed["dtb"] = np.ascontiguousarray(np.asarray(ins["L0_attn_dt_bias"], np.float32).reshape(32, 1))
    cw = np.asarray(ins["L0_attn_conv_w"], np.float32).reshape(DI, 4)
    feed["convw"] = np.ascontiguousarray(cw.reshape(16, 128, 4).transpose(1, 0, 2).reshape(128, 64))
    feed["convb"] = np.ascontiguousarray(np.asarray(ins["L0_attn_conv_b"], np.float32).reshape(16, 128).T)
    feed["normw"] = np.ascontiguousarray(np.asarray(ins["L0_attn_norm_w"], np.float32).reshape(16, 128).T)
    feed["mur"] = np.ascontiguousarray(np.asarray(ins["L0_ffn_mu_r"], np.float32).reshape(8, 128).T)
    feed["muk"] = np.ascontiguousarray(np.asarray(ins["L0_ffn_mu_k"], np.float32).reshape(8, 128).T)
    return feed


def out_to_host(o, NCH):
    return np.ascontiguousarray(np.asarray(o).reshape(NCH, D, L).transpose(0, 2, 1)).reshape(NCH * L, D)




_NC_CACHE = {}


def _get_nc(nch):
    if nch not in _NC_CACHE:
        nc = build_nc(nch)
        nc.finalize()
        _NC_CACHE[nch] = nc
    return _NC_CACHE[nch]


def kernel(**inputs):
    from concourse.bass_utils import run_bass_kernel_spmd
    nch = inputs["L0_attn_x"].shape[1] // L
    nc = _get_nc(nch)
    in_maps = [prep_core_inputs(inputs, b, nch) for b in range(8)]
    res = run_bass_kernel_spmd(nc, in_maps, core_ids=list(range(8)))
    outs = []
    for b in range(8):
        o = np.asarray(res.results[b]["out"], dtype=np.float32)
        outs.append(o.reshape(nch, D, L).transpose(0, 2, 1).reshape(nch * L, D))
    return np.ascontiguousarray(np.stack(outs, axis=0))
